# version 18
# baseline (speedup 1.0000x reference)
import contextlib
import numpy as np

import concourse.bass as bass
import concourse.mybir as mybir
from concourse.bass_utils import run_bass_kernel_spmd

F32 = mybir.dt.float32
BF = mybir.dt.bfloat16
ACT = mybir.ActivationFunctionType
ALU = mybir.AluOpType

D = 1024
T = 4096
TQ = 2048
NH = 16
R = 64
HPG = 8
NG = 2
NPR = HPG // 2
NB = 4
TB = 512
NTT = TB // 128
NUC = T // 128
NDC = D // 128
DFF = 4096
RK = 256
NFT = DFF // 128
EPS = 1e-5
QK_SCALE = 1.0 / 32.0
NM = NG * NB * HPG


class Plan:

    ENGINES = ("tensor", "scalar", "vector", "sync", "gpsimd")

    def __init__(self, nc):
        self.nc = nc
        self.ops = {e: [] for e in self.ENGINES}
        self.sems = {}
        self.cnt = {}

    def sem(self, name):
        if name not in self.sems:
            self.sems[name] = self.nc.alloc_semaphore(name)
            self.cnt[name] = 0
        return self.sems[name]

    def wait(self, eng, name, val):
        if val <= 0:
            return
        h = self.sem(name)
        self.ops[eng].append(lambda e, h=h, v=val: e.wait_ge(h, v))

    def do(self, eng, f, inc=None, k=1):
        if inc is None:
            self.ops[eng].append(f)
            return None
        h = self.sem(inc)
        self.cnt[inc] += k
        self.ops[eng].append(lambda e, f=f, h=h, k=k: f(e).then_inc(h, k))
        return self.cnt[inc]

    def nop_inc(self, eng, inc, k=1):
        h = self.sem(inc)
        self.cnt[inc] += k
        self.ops[eng].append(lambda e, h=h, k=k: e.nop().then_inc(h, k))
        return self.cnt[inc]

    def emit(self, block):
        secs = {
            "tensor": block.tensor, "scalar": block.scalar,
            "vector": block.vector, "sync": block.sync, "gpsimd": block.gpsimd,
        }
        for name in self.ENGINES:
            ops = self.ops[name]

            def runner(e, ops=ops):
                for f in ops:
                    f(e)
            secs[name](runner)
        self.ops = {e: [] for e in self.ENGINES}


def build_nc():
    nc = bass.Bass("TRN2", target_bir_lowering=False)

    x = nc.dram_tensor("x", [T, D], F32, kind="ExternalInput").ap()
    q_in = nc.dram_tensor("q", [NH, R, D], F32, kind="ExternalInput").ap()
    k_in = nc.dram_tensor("k", [NH, R, D], F32, kind="ExternalInput").ap()
    vo_in = nc.dram_tensor("vo", [NH, D, D], F32, kind="ExternalInput").ap()
    u1_in = nc.dram_tensor("u1", [D, RK], F32, kind="ExternalInput").ap()
    v1_in = nc.dram_tensor("v1", [RK, DFF], F32, kind="ExternalInput").ap()
    b1_in = nc.dram_tensor("b1", [DFF], F32, kind="ExternalInput").ap()
    u2_in = nc.dram_tensor("u2", [DFF, RK], F32, kind="ExternalInput").ap()
    v2_in = nc.dram_tensor("v2", [RK, D], F32, kind="ExternalInput").ap()
    b2_in = nc.dram_tensor("b2", [D], F32, kind="ExternalInput").ap()

    xts = nc.dram_tensor("xts", [T, D], BF, kind="Internal").ap()
    xfs = nc.dram_tensor("xfs", [D, T], BF, kind="Internal").ap()
    ats = nc.dram_tensor("ats", [TQ, D], F32, kind="Internal").ap()
    h1s = nc.dram_tensor("h1s", [TQ, D], BF, kind="Internal").ap()

    y = nc.dram_tensor("y", [TQ, D], F32, kind="ExternalOutput").ap()

    plan = Plan(nc)
    P = plan

    with contextlib.ExitStack() as stA:
        sb = lambda name, shape, dt: stA.enter_context(nc.sbuf_tensor(name, shape, dt))
        x_tok = sb("x_tok", [128, NUC * D], BF)
        fmc0 = sb("fmc0", [128, NDC * TB], BF)
        fmc1 = sb("fmc1", [128, NDC * TB], BF)
        xk_g = sb("xk_g", [128, NPR * T], BF)
        qT = sb("qT", [128, NPR * D], BF)
        att = sb("att", [128, NTT * D], F32)
        E = sb("E", [128, NUC * TB], BF)
        pT = sb("pT", [128, NDC * TB], BF)
        vo_sb = sb("vo_sb", [128, NDC * D], BF)
        xq = sb("xq", [128, NPR * TB], BF)
        shr = sb("shr", [128, 512], F32)
        stg0 = sb("stg0", [128, 1024], F32)
        stg1 = sb("stg1", [128, 1024], F32)
        onesb = sb("onesb", [128, 128], BF)
        idf = sb("idf", [128, 128], F32)
        csA = sb("csA", [128, 1], F32)

        kT = E

        ps_sc = stA.enter_context(nc.psum_tensor("ps_sc", [128, 2048], F32))
        ps_a = stA.enter_context(nc.psum_tensor("ps_a", [128, 512], F32))
        ps_b = stA.enter_context(nc.psum_tensor("ps_b", [128, 512], F32))
        ps_vo = stA.enter_context(nc.psum_tensor("ps_vo", [128, 1024], F32))

        P.do("gpsimd", lambda e: e.memset(onesb[:, :], 1.0), inc="s_ii")
        P.do("gpsimd", lambda e: e.memset(idf[:, :], 1.0), inc="s_ii")
        P.do("gpsimd", lambda e: e.memset(csA[:, :], QK_SCALE), inc="s_ii")
        P.wait("gpsimd", "s_ii", 3)
        P.do("gpsimd", lambda e: e.affine_select(
            idf[:, :], idf[:, :], [[1, 128]], ALU.is_equal, 0.0,
            base=0, channel_multiplier=-1), inc="s_id")

        for j in range(4):
            P.do("gpsimd", lambda e, d=xts[j * 1024:(j + 1) * 1024, :],
                 s=x[j * 1024:(j + 1) * 1024, :]: e.dma_start(d, s),
                 inc=f"s_xtw{j}", k=16)
            P.wait("gpsimd", f"s_xtw{j}", 16)
        for i in range(8):
            rows = x[i * 512:(i + 1) * 512, :].rearrange("(uc p) d -> p uc d", p=128)
            dst = x_tok.ap()[:, i * 4096:(i + 1) * 4096].rearrange(
                "p (uc d) -> p uc d", uc=4)
            P.do("gpsimd", lambda e, d=dst, s=rows: e.dma_start(d, s), inc="s_xc", k=16)

        for seg in range(4):
            P.wait("scalar", f"s_xtw{seg}", 16)
            for dc in range(NDC):
                p = dc % 2
                slot = E.ap()[:, 4096 + p * 4096 + seg * 1024:
                              4096 + p * 4096 + (seg + 1) * 1024]
                fts = f"s_fT{p}{seg}"
                fws = f"s_fw{p}{seg}"
                if dc >= 2:
                    P.wait("scalar", fws, 16 * (dc // 2))
                P.do("scalar", lambda e, d=slot,
                     s=xts[seg * 1024:(seg + 1) * 1024, dc * 128:(dc + 1) * 128]:
                     e.dma_start_transpose(d, s), inc=fts, k=16)
                P.wait("scalar", fts, P.cnt[fts])
                P.do("scalar", lambda e,
                     d=xfs[dc * 128:(dc + 1) * 128, seg * 1024:(seg + 1) * 1024],
                     s=slot: e.dma_start(d, s), inc=fws, k=16)

        def emit_qk_prep(g):
            kT_gate_done = [False]
            k_load = [NPR * 2 * g]
            for pr in range(NPR):
                for which, src_t, dst_t, scale in (
                        (0, q_in, qT, 1.0), (1, k_in, kT, QK_SCALE)):
                    kl = k_load[0]
                    k_load[0] += 1
                    stg = stg0 if kl % 2 == 0 else stg1
                    hid0 = g * HPG + 2 * pr
                    src2 = src_t[hid0:hid0 + 2, :, :].rearrange("two r d -> (two r) d")
                    if kl >= 2:
                        P.wait("sync", "s_qkT", 8 * (kl - 1))
                    P.do("sync", lambda e, d=stg.ap()[:, 0:1024], s=src2:
                         e.dma_start(d, s), inc=f"s_qkld{kl % 2}", k=16)
                    ldv = P.cnt[f"s_qkld{kl % 2}"]
                    for dc in range(NDC):
                        n_t = P.cnt.get("s_qkT", 0) + 1
                        psb = ps_a if n_t % 2 == 1 else ps_b
                        P.wait("tensor", f"s_qkld{kl % 2}", ldv)
                        if g > 0 and n_t == 64 * g + 1:
                            P.wait("tensor", "s_bigevac", 8 * HPG * NB * g)
                        P.wait("tensor", "s_qke", n_t - 2)
                        P.wait("tensor", "s_id", 1)
                        P.do("tensor", lambda e, o=psb.ap()[:, 0:128],
                             i=stg.ap()[:, dc * 128:(dc + 1) * 128]:
                             e.transpose(o, i, idf[:, :]), inc="s_qkT")
                        tv = P.cnt["s_qkT"]
                        P.wait("vector", "s_qkT", tv)
                        if which == 1 and not kT_gate_done[0]:
                            P.wait("vector", "s_bigdc", 8 * HPG * NB * g)
                            kT_gate_done[0] = True
                        col = pr * 1024 + dc * 128
                        if which == 0:
                            P.do("vector", lambda e, o=dst_t.ap()[:, col:col + 128],
                                 i=psb.ap()[:, 0:128]: e.tensor_copy(o, i),
                                 inc="s_qke")
                        else:
                            P.do("vector", lambda e, o=dst_t.ap()[:, col:col + 128],
                                 i=psb.ap()[:, 0:128]: e.tensor_scalar_mul(
                                     o, i, QK_SCALE), inc="s_qke")

        def emit_xk_proj(g):
            for uc in range(8):
                buf = fmc0 if uc % 2 == 0 else fmc1
                if g == 0:
                    P.wait("sync", f"s_fw0{uc // 2}", 64)
                    P.wait("sync", f"s_fw1{uc // 2}", 64)
                if uc >= 2:
                    P.wait("sync", "s_xkmm", 32 * g + 4 * (uc - 1))
                elif g > 0 and uc == 0:
                    P.wait("sync", "s_xkmm", 32 * g)
                elif g > 0 and uc == 1:
                    P.wait("sync", "s_xqmm", 16 * g)
                src = xfs[:, uc * 512:(uc + 1) * 512].rearrange(
                    "(dc p) t -> p dc t", p=128)
                dst = buf.ap().rearrange("p (dc t) -> p dc t", dc=NDC)
                fmsem = f"s_fmk{uc % 2}"
                P.do("sync", lambda e, d=dst, s=src: e.dma_start(d, s),
                     inc=fmsem, k=16)
                fv = P.cnt[fmsem]
                P.wait("tensor", fmsem, fv)
                P.wait("tensor", "s_qke", 64 * (g + 1))
                if 32 * g + 4 * uc > 0:
                    P.wait("tensor", "s_xke", 32 * g + 4 * uc)
                if uc == 0 and g > 0:
                    P.wait("tensor", "s_expgrp", 16 * HPG * NB * g)
                for pr in range(NPR):
                    for dc in range(NDC):
                        kcol = pr * 1024 + dc * 128
                        P.do("tensor", lambda e,
                             o=ps_sc.ap()[:, pr * 512:(pr + 1) * 512],
                             l=kT.ap()[:, kcol:kcol + 128],
                             r=buf.ap()[:, dc * 512:(dc + 1) * 512],
                             st=(dc == 0), sp=(dc == NDC - 1):
                             e.matmul(o, l, r, start=st, stop=sp),
                             inc="s_xkmm" if dc == NDC - 1 else None)
                    mv = P.cnt["s_xkmm"]
                    P.wait("vector", "s_xkmm", mv)
                    dcol = pr * T + uc * 512
                    P.do("vector", lambda e, o=xk_g.ap()[:, dcol:dcol + 512],
                         i=ps_sc.ap()[:, pr * 512:(pr + 1) * 512]:
                         e.tensor_copy(o, i), inc="s_xke")

        def emit_fmq_load(Bg):
            if Bg % NB == 0:
                P.wait("sync", "s_xkmm", 32 * (Bg // NB + 1))
            else:
                P.wait("sync", "s_xqmm", 4 * Bg)
            b = Bg % NB
            src = xfs[:, b * 512:b * 512 + 512].rearrange("(dc p) t -> p dc t", p=128)
            dst = fmc1.ap().rearrange("p (dc t) -> p dc t", dc=NDC)
            P.do("sync", lambda e, d=dst, s=src: e.dma_start(d, s), inc="s_fmq", k=16)

        def emit_xq_proj(Bg, voevac_gate):
            P.wait("tensor", "s_fmq", 16 * (Bg + 1))
            P.wait("tensor", "s_qke", 64 * (Bg // NB + 1))
            P.wait("tensor", "s_voevac", voevac_gate)
            for pr in range(NPR):
                if pr >= 2:
                    P.wait("tensor", "s_xqe", 4 * Bg + pr - 1)
                for dc in range(NDC):
                    qcol = pr * 1024 + dc * 128
                    P.do("tensor", lambda e,
                         o=ps_vo.ap()[:, (pr % 2) * 512:(pr % 2) * 512 + 512],
                         l=qT.ap()[:, qcol:qcol + 128],
                         r=fmc1.ap()[:, dc * 512:(dc + 1) * 512],
                         st=(dc == 0), sp=(dc == NDC - 1):
                         e.matmul(o, l, r, start=st, stop=sp),
                         inc="s_xqmm" if dc == NDC - 1 else None)
                mv = P.cnt["s_xqmm"]
                P.wait("vector", "s_xqmm", mv)
                P.do("vector", lambda e, o=xq.ap()[:, pr * 512:(pr + 1) * 512],
                     i=ps_vo.ap()[:, (pr % 2) * 512:(pr % 2) * 512 + 512]:
                     e.tensor_copy(o, i), inc="s_xqe")

        def emit_scores_gen(m):
            g, b, h = m // 32, (m // 8) % 4, m % 8
            pr, hh = h // 2, h % 2
            Bg = g * NB + b
            P.wait("tensor", "s_xqe", 4 * (Bg + 1))
            P.wait("tensor", "s_xke", 32 * (g + 1))
            for uj in range(NUC):
                j = uj // 2
                if uj % 2 == 0:
                    P.wait("tensor", "s_expgrp", m * 16 + j - 1)
                kcol = pr * T + uj * 128
                c0 = (uj % 4) * 512
                P.do("tensor", lambda e,
                     o=ps_sc.ap()[:, c0:c0 + 512],
                     l=xk_g.ap()[hh * 64:hh * 64 + 64, kcol:kcol + 128],
                     r=xq.ap()[hh * 64:hh * 64 + 64, pr * 512:(pr + 1) * 512]:
                     e.matmul(o, l, r, start=True, stop=True),
                     inc="s_scgrp" if uj % 2 == 1 else None)
                yield uj

        def emit_scores(m):
            for _ in emit_scores_gen(m):
                pass

        def emit_exp(m):
            g = m // 32
            for j in range(16):
                P.wait("scalar", "s_scgrp", m * 16 + j + 1)
                if j == 0:
                    P.wait("scalar", "s_bighalf", m)
                    if m % 32 == 0:
                        P.wait("scalar", "s_xkmm", 32 * (g + 1))
                elif j == 8:
                    P.wait("scalar", "s_bigdc", 8 * m)
                P.do("scalar", lambda e,
                     o=E.ap()[:, (2 * j) * 512:(2 * j + 2) * 512],
                     i=ps_sc.ap()[:, (j % 2) * 1024:(j % 2) * 1024 + 1024]:
                     e.activation(o, i, ACT.Exp), inc="s_expgrp")

        def emit_head(m, last_in_grp, last):
            g, b, h = m // 32, (m // 8) % 4, m % 8
            Bg = g * NB + b
            P.wait("tensor", "s_expgrp", 16 * (m + 1))
            P.wait("tensor", "s_bigevac", 8 * m)
            for uc in range(NUC):
                P.do("tensor", lambda e, o=ps_a.ap()[:, 0:512],
                     l=onesb[:, :], r=E.ap()[:, uc * 512:(uc + 1) * 512],
                     st=(uc == 0), sp=(uc == NUC - 1):
                     e.matmul(o, l, r, start=st, stop=sp),
                     inc="s_zmm" if uc == NUC - 1 else None)
            P.wait("vector", "s_zmm", m + 1)
            P.do("vector", lambda e: e.reciprocal(shr.ap()[:, 0:512],
                                                  ps_a.ap()[:, 0:512]),
                 inc="s_recip")
            for dt in range(NDC):
                psb = ps_b if dt % 2 == 0 else ps_a
                if dt == 0:
                    if m == 0:
                        P.wait("tensor", "s_xc", 128)
                    P.wait("tensor", "s_bigevac", 8 * m)
                elif dt == 1:
                    P.wait("tensor", "s_recip", m + 1)
                    P.wait("tensor", "s_bigevac", 8 * m)
                else:
                    P.wait("tensor", "s_bigevac", 8 * m + dt - 1)
                for uc in range(NUC):
                    xcol = uc * D + dt * 128
                    inc = None
                    if uc == NUC - 1:
                        inc = "s_bigdc"
                    elif dt == NDC - 1 and uc == 15:
                        inc = "s_bighalf"
                    P.do("tensor", lambda e, o=psb.ap()[:, 0:512],
                         l=x_tok.ap()[:, xcol:xcol + 128],
                         r=E.ap()[:, uc * 512:(uc + 1) * 512],
                         st=(uc == 0), sp=(uc == NUC - 1):
                         e.matmul(o, l, r, start=st, stop=sp), inc=inc)
                dv = P.cnt["s_bigdc"]
                P.wait("vector", "s_bigdc", dv)
                if dt == 0:
                    P.wait("vector", "s_recip", m + 1)
                P.do("vector", lambda e, o=pT.ap()[:, dt * 512:(dt + 1) * 512],
                     i=(ps_b if dt % 2 == 0 else ps_a).ap()[:, 0:512]:
                     e.tensor_tensor(o, i, shr.ap()[:, 0:512], ALU.mult),
                     inc="s_bigevac")
            did_xq = False
            if not last_in_grp and h == 7:
                emit_xq_proj(Bg + 1, 8 * m)
                did_xq = True

            def emit_vo_group(tt, eh):
                vg = m * 8 + tt * 2 + eh
                if tt == 0 and eh == 0:
                    P.wait("tensor", "s_bigevac", 8 * (m + 1))
                    P.wait("tensor", "s_vodma", 16 * (m + 1))
                if vg >= 2:
                    P.wait("tensor", "s_voevac", vg - 1)
                if did_xq and tt == 0:
                    P.wait("tensor", "s_xqe", 4 * (Bg + 2))
                for dc in range(NDC):
                    pcol = dc * 512 + tt * 128
                    vcol = dc * D + eh * 512
                    P.do("tensor", lambda e,
                         o=ps_vo.ap()[:, eh * 512:eh * 512 + 512],
                         l=pT.ap()[:, pcol:pcol + 128],
                         r=vo_sb.ap()[:, vcol:vcol + 512],
                         st=(dc == 0), sp=(dc == NDC - 1):
                         e.matmul(o, l, r, start=st, stop=sp),
                         inc="s_vogrp" if dc == NDC - 1 else None)
                vv = P.cnt["s_vogrp"]
                P.wait("vector", "s_vogrp", vv)
                if vg >= 1:
                    P.wait("vector", "s_voevac", vg)
                if h == 0:
                    P.wait("vector", "s_attspill", 16 * Bg)
                    if g == 1:
                        P.wait("vector", "s_attpre", 16 * (b + 1))
                acol = tt * 1024 + eh * 512
                if h == 0 and g == 0:
                    P.do("vector", lambda e, o=att.ap()[:, acol:acol + 512],
                         i=ps_vo.ap()[:, eh * 512:eh * 512 + 512]:
                         e.tensor_copy(o, i), inc="s_voevac")
                else:
                    P.do("vector", lambda e, o=att.ap()[:, acol:acol + 512],
                         i=ps_vo.ap()[:, eh * 512:eh * 512 + 512]:
                         e.tensor_tensor(o, i, o, ALU.add), inc="s_voevac")

            if not last_in_grp:
                sc = emit_scores_gen(m + 1)
            else:
                sc = None
            if sc is not None:
                for _ in range(4):
                    next(sc, None)
            for tt in range(NTT):
                for eh in range(2):
                    emit_vo_group(tt, eh)
                    if sc is not None:
                        for _ in range(4):
                            next(sc, None)
            if sc is not None:
                for _ in sc:
                    pass

        def emit_vo_dma(m):
            g, h = m // 32, m % 8
            hid = g * HPG + h
            if m > 0:
                P.wait("gpsimd", "s_vogrp", 8 * m)
            src = vo_in[hid, :, :].rearrange("(dc p) e -> p dc e", p=128)
            dst = vo_sb.ap().rearrange("p (dc e) -> p dc e", dc=NDC)
            P.do("gpsimd", lambda e, d=dst, s=src: e.dma_start(d, s),
                 inc="s_vodma", k=16)

        def emit_att_preload(b):
            P.wait("sync", "s_attspill", 16 * (NB + b))
            src = ats[b * 512:(b + 1) * 512, :].rearrange("(tt p) e -> p tt e", p=128)
            dst = att.ap().rearrange("p (tt e) -> p tt e", tt=NTT)
            P.do("sync", lambda e, d=dst, s=src: e.dma_start(d, s),
                 inc="s_attpre", k=16)

        def emit_att_spill(m):
            b = (m // 8) % 4
            P.wait("sync", "s_voevac", 8 * (m + 1))
            dst = ats[b * 512:(b + 1) * 512, :].rearrange("(tt p) e -> p tt e", p=128)
            src = att.ap().rearrange("p (tt e) -> p tt e", tt=NTT)
            P.do("sync", lambda e, d=dst, s=src: e.dma_start(d, s),
                 inc="s_attspill", k=16)

        for m in range(NM):
            g, b, h = m // 32, (m // 8) % 4, m % 8
            if h == 0 and b == 0:
                emit_qk_prep(g)
                emit_xk_proj(g)
                emit_fmq_load(g * NB)
                emit_xq_proj(g * NB, 8 * 32 * g)
                emit_vo_dma(m)
                emit_scores(m)
            if h == 0 and g == 1:
                emit_att_preload(b)
            emit_exp(m)
            last = (m == NM - 1)
            last_in_grp = (m % 32 == 31)
            emit_head(m, last_in_grp, last)
            if not last and (m + 1) % 32 != 0:
                emit_vo_dma(m + 1)
                if h == 7:
                    emit_fmq_load((m + 1) // 8)
            if h == 7:
                emit_att_spill(m)

        blockA = nc.Block()
        with blockA as blk:
            plan.emit(blk)

    with contextlib.ExitStack() as stB:
        sb = lambda name, shape, dt: stB.enter_context(nc.sbuf_tensor(name, shape, dt))
        arow = sb("arow", [128, NTT * D], F32)
        xa = sb("xa", [128, 2 * NTT * D], F32)
        h1 = sb("h1", [128, 2 * NTT * D], F32)
        h1b = sb("h1b", [128, NTT * D], BF)
        h1fm = sb("h1fm", [128, 2 * NDC * TB], BF)
        junkb = sb("junkb", [128, 2 * NTT * 1024], BF)
        U1sb = sb("U1sb", [128, NDC * RK], BF)
        V1sb = sb("V1sb", [128, 2 * DFF], BF)
        U2sb = sb("U2sb", [128, NFT * RK], BF)
        V2sb = sb("V2sb", [128, 2 * D], BF)
        b1sb = sb("b1sb", [128, NFT], F32)
        b2row = sb("b2row", [128, D], F32)
        Hc = [sb(f"Hc{i}", [128, 512], BF) for i in range(4)]
        p1sb = sb("p1sb", [128, 1024], BF)
        q2sb = sb("q2sb", [128, 1024], BF)
        outb = sb("outb", [128, NTT * D], F32)
        xfb = sb("xfb", [128, NTT * D], F32)
        st4 = sb("st4", [128, 96], F32)
        csB = sb("csB", [128, 2], F32)
        onesf = sb("onesf", [128, 128], F32)
        idbfB = sb("idbfB", [128, 128], BF)

        ps_h = [stB.enter_context(nc.psum_tensor(f"ps_h{i}", [128, 512], F32))
                for i in range(2)]
        ps_q2 = [stB.enter_context(nc.psum_tensor(f"ps_q2{i}", [128, 512], F32))
                 for i in range(2)]
        ps_o2 = [stB.enter_context(nc.psum_tensor(f"ps_o2{i}", [128, 512], F32))
                 for i in range(2)]
        ps_tb = [stB.enter_context(nc.psum_tensor(f"ps_tb{i}", [128, 1024], BF))
                 for i in range(2)]

        def pcol(b, c):
            return (b % 2) * (NTT * D) + c

        def xab(b, c):
            return xa.ap()[:, pcol(b, c.start):pcol(b, c.stop)]

        def h1v(b, c):
            return h1.ap()[:, pcol(b, c.start):pcol(b, c.stop)]

        def jkb1(c):
            return junkb[:, c.start:c.stop]

        def jkb3(c):
            return junkb[:, NTT * D + c.start:NTT * D + c.stop]

        def outv(c):
            return outb.ap()[:, c.start:c.stop]

        def xfbv(c):
            return xfb.ap()[:, c.start:c.stop]

        def h1fmv(b, dc):
            o = (b % 2) * (NDC * TB) + dc * 512
            return h1fm.ap()[:, o:o + 512]

        stb = lambda b, a, z: st4.ap()[:, (b % 4) * 24 + a:(b % 4) * 24 + z]
        s1c = lambda b, tt: stb(b, tt, tt + 1)
        s2c = lambda b, tt: stb(b, 4 + tt, 5 + tt)
        m4c = lambda b, tt: stb(b, 8 + tt, 9 + tt)
        r4c = lambda b, tt: stb(b, 16 + tt, 17 + tt)

        GATE = 16 * NG * NB

        P.wait("gpsimd", "s_attspill", GATE)
        P.do("gpsimd", lambda e: e.memset(csB[:, 0:1], EPS), inc="s_jj")
        P.do("gpsimd", lambda e: e.memset(csB[:, 1:2], -0.5), inc="s_jj")
        P.do("gpsimd", lambda e: e.memset(onesf[:, :], 1.0), inc="s_jj")
        P.do("gpsimd", lambda e: e.memset(idbfB[:, :], 1.0), inc="s_jj")
        P.wait("gpsimd", "s_jj", 4)
        P.do("gpsimd", lambda e: e.affine_select(
            idbfB[:, :], idbfB[:, :], [[1, 128]], ALU.is_equal, 0.0,
            base=0, channel_multiplier=-1), inc="s_idB")
        P.do("gpsimd", lambda e: e.dma_start(b2row.ap()[0:1, :], b2_in.rearrange(
            "(o d) -> o d", o=1)), inc="s_b2l", k=16)
        P.wait("tensor", "s_attspill", GATE)
        P.wait("tensor", "s_b2l", 16)
        P.wait("tensor", "s_jj", 4)
        for half in range(2):
            P.do("tensor", lambda e, half=half: e.matmul(
                ps_o2[half].ap()[:, 0:512], onesf[0:1, :],
                b2row.ap()[0:1, half * 512:half * 512 + 512],
                start=True, stop=True), inc="s_b2m")
        P.wait("vector", "s_b2m", 2)
        for half in range(2):
            P.do("vector", lambda e, half=half: e.tensor_copy(
                b2row.ap()[:, half * 512:half * 512 + 512],
                ps_o2[half].ap()[:, 0:512]), inc="s_id2")

        def wload(sem, dst, src, slow=False):
            P.do("gpsimd", lambda e, d=dst, s=src: e.dma_start(
                d, s, allow_slow_non_contiguous=slow), inc=sem, k=16)
        wload("s_wu1", U1sb.ap().rearrange("p (dc r) -> p dc r", dc=NDC),
              u1_in.rearrange("(dc p) r -> p dc r", p=128))
        wload("s_wv1", V1sb.ap().rearrange("p (rc f) -> p rc f", rc=2),
              v1_in.rearrange("(rc p) f -> p rc f", p=128))
        wload("s_wu2", U2sb.ap().rearrange("p (fc r) -> p fc r", fc=NFT),
              u2_in.rearrange("(fc p) r -> p fc r", p=128))
        wload("s_wb1", b1sb.ap(), b1_in.rearrange("(ft p) -> p ft", p=128),
              slow=True)
        wload("s_wv2", V2sb.ap().rearrange("p (rc d) -> p rc d", rc=2),
              v2_in.rearrange("(rc p) d -> p rc d", p=128))

        def ln_stats_vec(phase, b):
            tag = f"s_var{phase}"
            P.do("vector", lambda e, b=b: e.tensor_scalar_mul(
                stb(b, 8, 12), stb(b, 0, 4), 1.0 / D), inc="s_lns")
            P.do("vector", lambda e, b=b: e.tensor_scalar_mul(
                stb(b, 12, 16), stb(b, 4, 8), 1.0 / D), inc="s_lns")
            P.wait("vector", "s_lns", P.cnt["s_lns"])
            P.do("vector", lambda e, b=b: e.tensor_mul(
                stb(b, 16, 20), stb(b, 8, 12), stb(b, 8, 12)), inc="s_lns")
            P.wait("vector", "s_lns", P.cnt["s_lns"])
            P.do("vector", lambda e, b=b: e.tensor_sub(
                stb(b, 20, 24), stb(b, 12, 16), stb(b, 16, 20)), inc=tag)

        def ln_stats_scl(phase, b):
            tag = f"s_var{phase}"
            P.wait("scalar", tag, P.cnt[tag])
            P.wait("scalar", "s_id2", 2)
            P.do("scalar", lambda e, b=b: e.activation(
                stb(b, 12, 16), stb(b, 20, 24), ACT.Ln,
                bias=csB.ap()[:, 0:1]), inc="s_lnx")
            P.wait("scalar", "s_lnx", P.cnt["s_lnx"])
            P.do("scalar", lambda e, b=b: e.activation(
                stb(b, 16, 20), stb(b, 12, 16), ACT.Exp,
                scale=csB.ap()[:, 1:2]), inc=f"s_rstd{phase}")

        def emit_B1_loads(b):
            rows = slice(b * 512, (b + 1) * 512)
            if b == 0:
                P.wait("sync", "s_attspill", GATE)
            if b >= 2:
                P.wait("sync", "s_h1a", 4 * (b - 1))
            P.do("sync", lambda e, b=b,
                 d=xa.ap()[:, (b % 2) * 4096:(b % 2) * 4096 + 4096].rearrange(
                     "p (tt d) -> p tt d", tt=NTT),
                 s=x[rows, :].rearrange("(tt p) d -> p tt d", p=128):
                 e.dma_start(d, s), inc=f"s_xr{b % 2}", k=16)
            if b >= 1:
                P.wait("scalar", "s_xa", 4 * b)
            P.do("scalar", lambda e,
                 d=arow.ap().rearrange("p (tt e) -> p tt e", tt=NTT),
                 s=ats[rows, :].rearrange("(tt p) e -> p tt e", p=128):
                 e.dma_start(d, s), inc="s_ar", k=16)

        def b1_stt(b, tts):
            if tts[0] == 0:
                if b == 0:
                    P.wait("vector", "s_attspill", GATE)
                    P.wait("scalar", "s_attspill", GATE)
                P.wait("vector", f"s_xr{b % 2}", 16 * (b // 2 + 1))
                P.wait("vector", "s_ar", 16 * (b + 1))
                if b >= 2:
                    P.wait("vector", "s_h1a", 4 * (b - 1))
            for tt in tts:
                c = slice(tt * 1024, (tt + 1) * 1024)
                P.do("vector", lambda e, c=c, tt=tt, b=b: e.scalar_tensor_tensor(
                    xab(b, c), arow.ap()[:, c], 1.0, xab(b, c),
                    ALU.mult, ALU.add, accum_out=s1c(b, tt)), inc="s_xa")

        def b1_sq(b):
            if b >= 1:
                P.wait("vector", "s_sq1", 4 * b)
            for tt in range(NTT):
                c = slice(tt * 1024, (tt + 1) * 1024)
                P.wait("vector", "s_xa", 4 * b + tt + 1)
                P.do("vector", lambda e, c=c, tt=tt, b=b: e.scalar_tensor_tensor(
                    jkb1(c), xab(b, c), 1.0, xab(b, c),
                    ALU.mult, ALU.mult, accum_out=s2c(b, tt)), inc="s_sq1")
            P.wait("vector", "s_sq1", 4 * (b + 1))
            ln_stats_vec(1, b)

        def b1_h1b(b):
            P.wait("vector", "s_rstd1", b + 1)
            if b >= 1:
                P.wait("vector", "s_htm", 32 * b)
            for tt in range(NTT):
                c = slice(tt * 1024, (tt + 1) * 1024)
                P.do("vector", lambda e, c=c, tt=tt, b=b: e.tensor_scalar(
                    h1b.ap()[:, c], xab(b, c), m4c(b, tt), r4c(b, tt),
                    ALU.subtract, ALU.mult), inc="s_h1")

        def b1_h1(b):
            if b >= 2:
                P.wait("vector", "s_tev", 8 * (b - 1))
            for tt in range(NTT):
                c = slice(tt * 1024, (tt + 1) * 1024)
                P.do("vector", lambda e, c=c, tt=tt, b=b: e.tensor_scalar(
                    h1v(b, c), xab(b, c), m4c(b, tt), r4c(b, tt),
                    ALU.subtract, ALU.mult), inc="s_h1a")
            P.wait("vector", "s_h1a", 4 * (b + 1))
            P.wait("vector", "s_id2", 2)
            for tt in range(NTT):
                c = slice(tt * 1024, (tt + 1) * 1024)
                P.do("vector", lambda e, c=c, b=b: e.tensor_tensor(
                    h1v(b, c), b2row.ap()[:, 0:1024],
                    h1v(b, c), ALU.add), inc="s_h1b2")
        def emit_B1T(b):
            P.wait("tensor", "s_idB", 1)
            for dc in range(NDC):
                for tt in range(NTT):
                    ti = dc * NTT + tt
                    pst = ps_tb[ti % 2]
                    P.wait("tensor", "s_h1", 4 * b + tt + 1)
                    if 32 * b + ti >= 2:
                        P.wait("tensor", "s_hfe", 32 * b + ti - 1)
                    hcol = tt * 1024 + dc * 128
                    P.do("tensor", lambda e, pst=pst, hcol=hcol: e.transpose(
                        pst.ap()[:, 0:128], h1b.ap()[:, hcol:hcol + 128],
                        idbfB[:, :]), inc="s_htm")
                    tv = P.cnt["s_htm"]
                    P.wait("vector", "s_htm", tv)
                    if ti == 0 and b >= 2:
                        P.wait("vector", "s_p1", 2 * (b - 1))
                    fcol = (b % 2) * (NDC * TB) + dc * 512 + tt * 128
                    P.do("vector", lambda e, pst=pst, fcol=fcol: e.tensor_copy(
                        h1fm.ap()[:, fcol:fcol + 128], pst.ap()[:, 0:128]),
                        inc="s_hfe")

        def emit_B2p1(b):
            if b == 0:
                P.wait("tensor", "s_wu1", 16)
            if b > 0:
                P.wait("tensor", "s_q2e", 2 * b)
            for rt in range(2):
                for dc in range(NDC):
                    if rt == 0:
                        P.wait("tensor", "s_hfe", 32 * b + 4 * (dc + 1))
                    col = dc * RK + rt * 128
                    P.do("tensor", lambda e, o=ps_q2[rt].ap()[:, 0:512],
                         l=U1sb.ap()[:, col:col + 128], b=b, dc=dc:
                         e.matmul(o, l, h1fmv(b, dc), start=(dc == 0),
                                  stop=(dc == NDC - 1)),
                         inc="s_p1" if dc == NDC - 1 else None)
                pv = P.cnt["s_p1"]
                P.wait("vector", "s_p1", pv)
                P.do("vector", lambda e, rt=rt: e.tensor_copy(
                    p1sb.ap()[:, rt * 512:(rt + 1) * 512],
                    ps_q2[rt].ap()[:, 0:512]), inc="s_p1e")

        def b3_o2mm(b, oi):
            tt, dh = oi // 2, oi % 2
            if oi == 0:
                P.wait("tensor", "s_q2e", 2 * (b + 1))
                if b == 0:
                    P.wait("tensor", "s_wv2", 16)
            if b == 0 and oi < 2:
                P.wait("tensor", "s_id2", 2)
            if 8 * b + oi >= 2:
                P.wait("tensor", "s_tev", 8 * b + oi - 1)
            pso = ps_o2[oi % 2]
            for r2c in range(2):
                qcol = r2c * 512 + tt * 128
                vcol = r2c * D + dh * 512
                P.do("tensor", lambda e, o=pso.ap()[:, 0:512],
                     l=q2sb.ap()[:, qcol:qcol + 128],
                     r=V2sb.ap()[:, vcol:vcol + 512],
                     st=(r2c == 0), sp=(r2c == 1):
                     e.matmul(o, l, r, start=st, stop=sp),
                     inc="s_o2m" if r2c == 1 else None)

        def b3_stats_tt(b, tt):
            c = slice(tt * 1024, (tt + 1) * 1024)
            P.wait("vector", "s_tev", 8 * b + 2 * tt + 2)
            P.do("vector", lambda e, c=c, tt=tt, b=b: e.reduce_sum(
                s1c(b, tt), xfbv(c), axis=mybir.AxisListType.X), inc="s_r2")
            if tt == 0 and b >= 1:
                P.wait("vector", "s_sq2", 4 * b)
            P.do("vector", lambda e, c=c, tt=tt, b=b: e.scalar_tensor_tensor(
                jkb3(c), xfbv(c), 1.0, xfbv(c),
                ALU.mult, ALU.mult, accum_out=s2c(b, tt)), inc="s_sq2")

        def b3_evac(b, oi, defer=False):
            tt, dh = oi // 2, oi % 2
            P.wait("vector", "s_o2m", 8 * b + oi + 1)
            if oi == 0:
                P.wait("vector", "s_h1b2", 4 * (b + 1))
                if b >= 1:
                    P.wait("vector", "s_ln2", 4 * b)
            xc = tt * 1024 + dh * 512
            P.do("vector", lambda e, b=b, oi=oi, xc=xc: e.scalar_tensor_tensor(
                xfb.ap()[:, xc:xc + 512], ps_o2[oi % 2].ap()[:, 0:512], 1.0,
                h1.ap()[:, (b % 2) * 4096 + xc:(b % 2) * 4096 + xc + 512],
                ALU.mult, ALU.add), inc="s_tev")
            if dh == 1 and not defer:
                b3_stats_tt(b, tt)

        def b3_ln2stats(b):
            P.wait("vector", "s_r2", 4 * (b + 1))
            P.wait("vector", "s_sq2", 4 * (b + 1))
            ln_stats_vec(2, b)

        def b3_ln2apply(b):
            P.wait("vector", "s_rstd2", b + 1)
            if b >= 1:
                P.wait("vector", "s_y", 16 * b)
            for tt in range(NTT):
                c = slice(tt * 1024, (tt + 1) * 1024)
                P.do("vector", lambda e, c=c, tt=tt, b=b: e.tensor_scalar(
                    outv(c), xfbv(c), m4c(b, tt), r4c(b, tt),
                    ALU.subtract, ALU.mult), inc="s_ln2")

        def b3_store(b):
            rows = slice(b * 512, (b + 1) * 512)
            P.wait("gpsimd", "s_ln2", 4 * (b + 1))
            P.do("gpsimd", lambda e,
                 d=y[rows, :].rearrange("(tt p) d -> p tt d", p=128),
                 s=outb.ap().rearrange("p (tt d) -> p tt d", tt=NTT):
                 e.dma_start(d, s), inc="s_y", k=16)

        def emit_B23(b):
            prev, nxt = b - 1, b + 1
            if prev >= 0:
                b3_evac(prev, 0)
                b3_evac(prev, 1)
            if nxt < NB:
                b1_stt(nxt, [0, 1])
            if prev >= 0:
                b3_evac(prev, 2)
                b3_evac(prev, 3)
            if nxt < NB:
                b1_stt(nxt, [2, 3])
                b1_sq(nxt)
            if prev >= 0:
                b3_evac(prev, 4)
            if nxt < NB:
                b1_h1b(nxt)
            if prev >= 0:
                b3_evac(prev, 5)
                b3_evac(prev, 6)
                b3_evac(prev, 7)
                b3_ln2stats(prev)
            for ft in range(NFT):
                hb = ps_h[ft % 2]
                hc = Hc[ft % 4]
                if ft == 0:
                    P.wait("tensor", "s_p1e", 2 * (b + 1))
                    if b == 0:
                        P.wait("tensor", "s_wv1", 16)
                        P.wait("tensor", "s_wu2", 16)
                        P.wait("scalar", "s_wb1", 16)
                if ft >= 2:
                    P.wait("tensor", "s_gelu", 32 * b + ft - 1)
                for rc in range(2):
                    col = rc * DFF + ft * 128
                    P.do("tensor", lambda e, o=hb.ap()[:, 0:512],
                         l=V1sb.ap()[:, col:col + 128],
                         r=p1sb.ap()[:, rc * 512:(rc + 1) * 512],
                         st=(rc == 0), sp=(rc == 1):
                         e.matmul(o, l, r, start=st, stop=sp),
                         inc="s_hmm" if rc == 1 else None)
                hv = P.cnt["s_hmm"]
                if ft == 24 and nxt < NB:
                    ln_stats_scl(1, nxt)
                P.wait("scalar", "s_hmm", hv)
                if ft >= 4 or b >= 1:
                    P.wait("scalar", "s_q2m", 64 * b + 2 * ft - 6)
                P.do("scalar", lambda e, hc=hc, hb=hb, ft=ft: e.activation(
                    hc[:, :], hb.ap()[:, 0:512], ACT.Gelu_apprx_tanh,
                    bias=b1sb.ap()[:, ft:ft + 1]), inc="s_gelu")
                if prev >= 0 and ft % 4 == 2:
                    b3_o2mm(prev, ft // 4)
                if ft >= 1:
                    qft = ft - 1
                    P.wait("tensor", "s_gelu", 32 * b + qft + 1)
                    for r2t in range(2):
                        col = qft * RK + r2t * 128
                        P.do("tensor", lambda e, o=ps_q2[r2t].ap()[:, 0:512],
                             l=U2sb.ap()[:, col:col + 128], r=Hc[qft % 4][:, :],
                             st=(qft == 0):
                             e.matmul(o, l, r, start=st, stop=False),
                             inc="s_q2m")
            P.wait("tensor", "s_gelu", 32 * b + 32)
            for r2t in range(2):
                col = 31 * RK + r2t * 128
                P.do("tensor", lambda e, o=ps_q2[r2t].ap()[:, 0:512],
                     l=U2sb.ap()[:, col:col + 128], r=Hc[31 % 4][:, :]:
                     e.matmul(o, l, r, start=False, stop=True),
                     inc="s_q2m")
            if prev >= 0:
                ln_stats_scl(2, prev)

        def emit_B23_tail(b):
            prev, nxt = b - 1, b + 1
            P.wait("vector", "s_q2m", 64 * (b + 1))
            if prev >= 0:
                P.wait("vector", "s_o2m", 8 * b)
            for r2t in range(2):
                P.do("vector", lambda e, r2t=r2t: e.tensor_copy(
                    q2sb.ap()[:, r2t * 512:(r2t + 1) * 512],
                    ps_q2[r2t].ap()[:, 0:512]), inc="s_q2e")
            if prev >= 0:
                b3_ln2apply(prev)
            if nxt < NB:
                b1_h1(nxt)
            if prev >= 0:
                b3_store(prev)

        def emit_B3tail():
            b = NB - 1
            for oi in range(8):
                b3_o2mm(b, oi)
            for oi in range(8):
                b3_evac(b, oi, defer=True)
            for tt in range(NTT):
                b3_stats_tt(b, tt)
            b3_ln2stats(b)
            b3_ln2apply(b)
            ln_stats_scl(2, b)
            b3_store(b)

        emit_B1_loads(0)
        b1_stt(0, [0, 1, 2, 3])
        b1_sq(0)
        ln_stats_scl(1, 0)
        b1_h1b(0)
        emit_B1T(0)
        b1_h1(0)
        for b in range(NB):
            emit_B2p1(b)
            if b + 1 < NB:
                emit_B1_loads(b + 1)
            emit_B23(b)
            if b + 1 < NB:
                emit_B1T(b + 1)
            emit_B23_tail(b)
        emit_B3tail()

        P.wait("sync", "s_y", 16 * NB)

        blockB = nc.Block()
        with blockB as blk:
            plan.emit(blk)

    return nc


_cache = {}


def kernel(L0_attn_x, L0_attn_Q, L0_attn_K, L0_attn_VO,
           L0_ffn_U1, L0_ffn_V1, L0_ffn_b1, L0_ffn_U2, L0_ffn_V2, L0_ffn_b2):
    xs = np.ascontiguousarray(np.asarray(L0_attn_x, dtype=np.float32))
    Q = np.ascontiguousarray(np.asarray(L0_attn_Q, dtype=np.float32))
    K = np.ascontiguousarray(np.asarray(L0_attn_K, dtype=np.float32))
    VO = np.ascontiguousarray(np.asarray(L0_attn_VO, dtype=np.float32))
    U1 = np.ascontiguousarray(np.asarray(L0_ffn_U1, dtype=np.float32))
    V1 = np.ascontiguousarray(np.asarray(L0_ffn_V1, dtype=np.float32))
    b1 = np.ascontiguousarray(np.asarray(L0_ffn_b1, dtype=np.float32))
    U2 = np.ascontiguousarray(np.asarray(L0_ffn_U2, dtype=np.float32))
    V2 = np.ascontiguousarray(np.asarray(L0_ffn_V2, dtype=np.float32))
    b2 = np.ascontiguousarray(np.asarray(L0_ffn_b2, dtype=np.float32))

    if "nc" not in _cache:
        _cache["nc"] = build_nc()
    nc = _cache["nc"]

    in_maps = []
    for c in range(8):
        bidx, off = c // 2, (c % 2) * TQ
        xb = np.ascontiguousarray(
            np.concatenate([xs[bidx, off:], xs[bidx, :off]], axis=0))
        in_maps.append({
            "x": xb, "q": Q, "k": K, "vo": VO,
            "u1": U1, "v1": V1, "b1": b1, "u2": U2, "v2": V2, "b2": b2,
        })
    res = run_bass_kernel_spmd(nc, in_maps, core_ids=list(range(8)))
    out = np.empty((4, T, D), dtype=np.float32)
    for c in range(8):
        bidx, off = c // 2, (c % 2) * TQ
        out[bidx, off:off + TQ] = res.results[c]["y"]
    return out


# version 19
# speedup vs baseline: 1.0049x; 1.0049x over previous
import contextlib
import numpy as np

import concourse.bass as bass
import concourse.mybir as mybir
from concourse.bass_utils import run_bass_kernel_spmd

F32 = mybir.dt.float32
BF = mybir.dt.bfloat16
ACT = mybir.ActivationFunctionType
ALU = mybir.AluOpType

D = 1024
T = 4096
TQ = 2048
NH = 16
R = 64
HPG = 8
NG = 2
NPR = HPG // 2
NB = 4
TB = 512
NTT = TB // 128
NUC = T // 128
NDC = D // 128
DFF = 4096
RK = 256
NFT = DFF // 128
EPS = 1e-5
QK_SCALE = 1.0 / 32.0
NM = NG * NB * HPG


class Plan:

    ENGINES = ("tensor", "scalar", "vector", "sync", "gpsimd")

    def __init__(self, nc):
        self.nc = nc
        self.ops = {e: [] for e in self.ENGINES}
        self.sems = {}
        self.cnt = {}

    def sem(self, name):
        if name not in self.sems:
            self.sems[name] = self.nc.alloc_semaphore(name)
            self.cnt[name] = 0
        return self.sems[name]

    def wait(self, eng, name, val):
        if val <= 0:
            return
        h = self.sem(name)
        self.ops[eng].append(lambda e, h=h, v=val: e.wait_ge(h, v))

    def do(self, eng, f, inc=None, k=1):
        if inc is None:
            self.ops[eng].append(f)
            return None
        h = self.sem(inc)
        self.cnt[inc] += k
        self.ops[eng].append(lambda e, f=f, h=h, k=k: f(e).then_inc(h, k))
        return self.cnt[inc]

    def nop_inc(self, eng, inc, k=1):
        h = self.sem(inc)
        self.cnt[inc] += k
        self.ops[eng].append(lambda e, h=h, k=k: e.nop().then_inc(h, k))
        return self.cnt[inc]

    def emit(self, block):
        secs = {
            "tensor": block.tensor, "scalar": block.scalar,
            "vector": block.vector, "sync": block.sync, "gpsimd": block.gpsimd,
        }
        for name in self.ENGINES:
            ops = self.ops[name]

            def runner(e, ops=ops):
                for f in ops:
                    f(e)
            secs[name](runner)
        self.ops = {e: [] for e in self.ENGINES}


def build_nc():
    nc = bass.Bass("TRN2", target_bir_lowering=False)

    x = nc.dram_tensor("x", [T, D], F32, kind="ExternalInput").ap()
    q_in = nc.dram_tensor("q", [NH, R, D], F32, kind="ExternalInput").ap()
    k_in = nc.dram_tensor("k", [NH, R, D], F32, kind="ExternalInput").ap()
    vo_in = nc.dram_tensor("vo", [NH, D, D], F32, kind="ExternalInput").ap()
    u1_in = nc.dram_tensor("u1", [D, RK], F32, kind="ExternalInput").ap()
    v1_in = nc.dram_tensor("v1", [RK, DFF], F32, kind="ExternalInput").ap()
    b1_in = nc.dram_tensor("b1", [DFF], F32, kind="ExternalInput").ap()
    u2_in = nc.dram_tensor("u2", [DFF, RK], F32, kind="ExternalInput").ap()
    v2_in = nc.dram_tensor("v2", [RK, D], F32, kind="ExternalInput").ap()
    b2_in = nc.dram_tensor("b2", [D], F32, kind="ExternalInput").ap()

    xts = nc.dram_tensor("xts", [T, D], BF, kind="Internal").ap()
    xfs = nc.dram_tensor("xfs", [D, T], BF, kind="Internal").ap()
    ats = nc.dram_tensor("ats", [TQ, D], F32, kind="Internal").ap()
    h1s = nc.dram_tensor("h1s", [TQ, D], BF, kind="Internal").ap()

    y = nc.dram_tensor("y", [TQ, D], F32, kind="ExternalOutput").ap()

    plan = Plan(nc)
    P = plan

    with contextlib.ExitStack() as stA:
        sb = lambda name, shape, dt: stA.enter_context(nc.sbuf_tensor(name, shape, dt))
        x_tok = sb("x_tok", [128, NUC * D], BF)
        fmc0 = sb("fmc0", [128, NDC * TB], BF)
        fmc1 = sb("fmc1", [128, NDC * TB], BF)
        xk_g = sb("xk_g", [128, NPR * T], BF)
        qT = sb("qT", [128, NPR * D], BF)
        att = sb("att", [128, NTT * D], F32)
        E = sb("E", [128, NUC * TB], BF)
        pT = sb("pT", [128, NDC * TB], BF)
        vo_sb = sb("vo_sb", [128, NDC * D], BF)
        xq = sb("xq", [128, NPR * TB], BF)
        shr = sb("shr", [128, 512], F32)
        stg0 = sb("stg0", [128, 1024], F32)
        stg1 = sb("stg1", [128, 1024], F32)
        onesb = sb("onesb", [128, 128], BF)
        idf = sb("idf", [128, 128], F32)
        csA = sb("csA", [128, 1], F32)

        kT = E

        ps_sc = stA.enter_context(nc.psum_tensor("ps_sc", [128, 2048], F32))
        ps_a = stA.enter_context(nc.psum_tensor("ps_a", [128, 512], F32))
        ps_b = stA.enter_context(nc.psum_tensor("ps_b", [128, 512], F32))
        ps_vo = stA.enter_context(nc.psum_tensor("ps_vo", [128, 1024], F32))

        P.do("gpsimd", lambda e: e.memset(onesb[:, :], 1.0), inc="s_ii")
        P.do("gpsimd", lambda e: e.memset(idf[:, :], 1.0), inc="s_ii")
        P.do("gpsimd", lambda e: e.memset(csA[:, :], QK_SCALE), inc="s_ii")
        P.wait("gpsimd", "s_ii", 3)
        P.do("gpsimd", lambda e: e.affine_select(
            idf[:, :], idf[:, :], [[1, 128]], ALU.is_equal, 0.0,
            base=0, channel_multiplier=-1), inc="s_id")

        for j in range(4):
            P.do("gpsimd", lambda e, d=xts[j * 1024:(j + 1) * 1024, :],
                 s=x[j * 1024:(j + 1) * 1024, :]: e.dma_start(d, s),
                 inc=f"s_xtw{j}", k=16)
            P.wait("gpsimd", f"s_xtw{j}", 16)
        for i in range(8):
            rows = x[i * 512:(i + 1) * 512, :].rearrange("(uc p) d -> p uc d", p=128)
            dst = x_tok.ap()[:, i * 4096:(i + 1) * 4096].rearrange(
                "p (uc d) -> p uc d", uc=4)
            P.do("gpsimd", lambda e, d=dst, s=rows: e.dma_start(d, s), inc="s_xc", k=16)

        for seg in range(4):
            P.wait("scalar", f"s_xtw{seg}", 16)
            for dc in range(NDC):
                if seg > 0:
                    P.wait("scalar", f"s_fw{dc}", 16 * seg)
                P.do("scalar", lambda e,
                     d=E.ap()[:, 4096 + dc * 1024:4096 + (dc + 1) * 1024],
                     s=xts[seg * 1024:(seg + 1) * 1024, dc * 128:(dc + 1) * 128]:
                     e.dma_start_transpose(d, s), inc=f"s_fT{dc}", k=16)
            for dc in range(NDC):
                P.wait("scalar", f"s_fT{dc}", 16 * (seg + 1))
                P.do("scalar", lambda e,
                     d=xfs[dc * 128:(dc + 1) * 128, seg * 1024:(seg + 1) * 1024],
                     s=E.ap()[:, 4096 + dc * 1024:4096 + (dc + 1) * 1024]:
                     e.dma_start(d, s), inc=f"s_fw{dc}", k=16)

        def emit_qk_prep(g):
            kT_gate_done = [False]
            k_load = [NPR * 2 * g]
            for pr in range(NPR):
                for which, src_t, dst_t, scale in (
                        (0, q_in, qT, 1.0), (1, k_in, kT, QK_SCALE)):
                    kl = k_load[0]
                    k_load[0] += 1
                    stg = stg0 if kl % 2 == 0 else stg1
                    hid0 = g * HPG + 2 * pr
                    src2 = src_t[hid0:hid0 + 2, :, :].rearrange("two r d -> (two r) d")
                    if kl >= 2:
                        P.wait("sync", "s_qkT", 8 * (kl - 1))
                    P.do("sync", lambda e, d=stg.ap()[:, 0:1024], s=src2:
                         e.dma_start(d, s), inc=f"s_qkld{kl % 2}", k=16)
                    ldv = P.cnt[f"s_qkld{kl % 2}"]
                    for dc in range(NDC):
                        n_t = P.cnt.get("s_qkT", 0) + 1
                        psb = ps_a if n_t % 2 == 1 else ps_b
                        P.wait("tensor", f"s_qkld{kl % 2}", ldv)
                        if g > 0 and n_t == 64 * g + 1:
                            P.wait("tensor", "s_bigevac", 8 * HPG * NB * g)
                        P.wait("tensor", "s_qke", n_t - 2)
                        P.wait("tensor", "s_id", 1)
                        P.do("tensor", lambda e, o=psb.ap()[:, 0:128],
                             i=stg.ap()[:, dc * 128:(dc + 1) * 128]:
                             e.transpose(o, i, idf[:, :]), inc="s_qkT")
                        tv = P.cnt["s_qkT"]
                        P.wait("vector", "s_qkT", tv)
                        if which == 1 and not kT_gate_done[0]:
                            P.wait("vector", "s_bigdc", 8 * HPG * NB * g)
                            kT_gate_done[0] = True
                        col = pr * 1024 + dc * 128
                        if which == 0:
                            P.do("vector", lambda e, o=dst_t.ap()[:, col:col + 128],
                                 i=psb.ap()[:, 0:128]: e.tensor_copy(o, i),
                                 inc="s_qke")
                        else:
                            P.do("vector", lambda e, o=dst_t.ap()[:, col:col + 128],
                                 i=psb.ap()[:, 0:128]: e.tensor_scalar_mul(
                                     o, i, QK_SCALE), inc="s_qke")

        def emit_xk_proj(g):
            for uc in range(8):
                buf = fmc0 if uc % 2 == 0 else fmc1
                if g == 0:
                    for dc in range(NDC):
                        P.wait("sync", f"s_fw{dc}", 16 * (uc // 2 + 1))
                if uc >= 2:
                    P.wait("sync", "s_xkmm", 32 * g + 4 * (uc - 1))
                elif g > 0 and uc == 0:
                    P.wait("sync", "s_xkmm", 32 * g)
                elif g > 0 and uc == 1:
                    P.wait("sync", "s_xqmm", 16 * g)
                src = xfs[:, uc * 512:(uc + 1) * 512].rearrange(
                    "(dc p) t -> p dc t", p=128)
                dst = buf.ap().rearrange("p (dc t) -> p dc t", dc=NDC)
                fmsem = f"s_fmk{uc % 2}"
                P.do("sync", lambda e, d=dst, s=src: e.dma_start(d, s),
                     inc=fmsem, k=16)
                fv = P.cnt[fmsem]
                P.wait("tensor", fmsem, fv)
                P.wait("tensor", "s_qke", 64 * (g + 1))
                if 32 * g + 4 * uc > 0:
                    P.wait("tensor", "s_xke", 32 * g + 4 * uc)
                if uc == 0 and g > 0:
                    P.wait("tensor", "s_expgrp", 16 * HPG * NB * g)
                for pr in range(NPR):
                    for dc in range(NDC):
                        kcol = pr * 1024 + dc * 128
                        P.do("tensor", lambda e,
                             o=ps_sc.ap()[:, pr * 512:(pr + 1) * 512],
                             l=kT.ap()[:, kcol:kcol + 128],
                             r=buf.ap()[:, dc * 512:(dc + 1) * 512],
                             st=(dc == 0), sp=(dc == NDC - 1):
                             e.matmul(o, l, r, start=st, stop=sp),
                             inc="s_xkmm" if dc == NDC - 1 else None)
                    mv = P.cnt["s_xkmm"]
                    P.wait("vector", "s_xkmm", mv)
                    dcol = pr * T + uc * 512
                    P.do("vector", lambda e, o=xk_g.ap()[:, dcol:dcol + 512],
                         i=ps_sc.ap()[:, pr * 512:(pr + 1) * 512]:
                         e.tensor_copy(o, i), inc="s_xke")

        def emit_fmq_load(Bg):
            if Bg % NB == 0:
                P.wait("sync", "s_xkmm", 32 * (Bg // NB + 1))
            else:
                P.wait("sync", "s_xqmm", 4 * Bg)
            b = Bg % NB
            src = xfs[:, b * 512:b * 512 + 512].rearrange("(dc p) t -> p dc t", p=128)
            dst = fmc1.ap().rearrange("p (dc t) -> p dc t", dc=NDC)
            P.do("sync", lambda e, d=dst, s=src: e.dma_start(d, s), inc="s_fmq", k=16)

        def emit_xq_proj(Bg, voevac_gate):
            P.wait("tensor", "s_fmq", 16 * (Bg + 1))
            P.wait("tensor", "s_qke", 64 * (Bg // NB + 1))
            P.wait("tensor", "s_voevac", voevac_gate)
            for pr in range(NPR):
                if pr >= 2:
                    P.wait("tensor", "s_xqe", 4 * Bg + pr - 1)
                for dc in range(NDC):
                    qcol = pr * 1024 + dc * 128
                    P.do("tensor", lambda e,
                         o=ps_vo.ap()[:, (pr % 2) * 512:(pr % 2) * 512 + 512],
                         l=qT.ap()[:, qcol:qcol + 128],
                         r=fmc1.ap()[:, dc * 512:(dc + 1) * 512],
                         st=(dc == 0), sp=(dc == NDC - 1):
                         e.matmul(o, l, r, start=st, stop=sp),
                         inc="s_xqmm" if dc == NDC - 1 else None)
                mv = P.cnt["s_xqmm"]
                P.wait("vector", "s_xqmm", mv)
                P.do("vector", lambda e, o=xq.ap()[:, pr * 512:(pr + 1) * 512],
                     i=ps_vo.ap()[:, (pr % 2) * 512:(pr % 2) * 512 + 512]:
                     e.tensor_copy(o, i), inc="s_xqe")

        def emit_scores_gen(m):
            g, b, h = m // 32, (m // 8) % 4, m % 8
            pr, hh = h // 2, h % 2
            Bg = g * NB + b
            P.wait("tensor", "s_xqe", 4 * (Bg + 1))
            P.wait("tensor", "s_xke", 32 * (g + 1))
            for uj in range(NUC):
                j = uj // 2
                if uj % 2 == 0:
                    P.wait("tensor", "s_expgrp", m * 16 + j - 1)
                kcol = pr * T + uj * 128
                c0 = (uj % 4) * 512
                P.do("tensor", lambda e,
                     o=ps_sc.ap()[:, c0:c0 + 512],
                     l=xk_g.ap()[hh * 64:hh * 64 + 64, kcol:kcol + 128],
                     r=xq.ap()[hh * 64:hh * 64 + 64, pr * 512:(pr + 1) * 512]:
                     e.matmul(o, l, r, start=True, stop=True),
                     inc="s_scgrp" if uj % 2 == 1 else None)
                yield uj

        def emit_scores(m):
            for _ in emit_scores_gen(m):
                pass

        def emit_exp(m):
            g = m // 32
            for j in range(16):
                P.wait("scalar", "s_scgrp", m * 16 + j + 1)
                if j == 0:
                    P.wait("scalar", "s_bighalf", m)
                    if m % 32 == 0:
                        P.wait("scalar", "s_xkmm", 32 * (g + 1))
                elif j == 8:
                    P.wait("scalar", "s_bigdc", 8 * m)
                P.do("scalar", lambda e,
                     o=E.ap()[:, (2 * j) * 512:(2 * j + 2) * 512],
                     i=ps_sc.ap()[:, (j % 2) * 1024:(j % 2) * 1024 + 1024]:
                     e.activation(o, i, ACT.Exp), inc="s_expgrp")

        def emit_head(m, last_in_grp, last):
            g, b, h = m // 32, (m // 8) % 4, m % 8
            Bg = g * NB + b
            P.wait("tensor", "s_expgrp", 16 * (m + 1))
            P.wait("tensor", "s_bigevac", 8 * m)
            for uc in range(NUC):
                P.do("tensor", lambda e, o=ps_a.ap()[:, 0:512],
                     l=onesb[:, :], r=E.ap()[:, uc * 512:(uc + 1) * 512],
                     st=(uc == 0), sp=(uc == NUC - 1):
                     e.matmul(o, l, r, start=st, stop=sp),
                     inc="s_zmm" if uc == NUC - 1 else None)
            P.wait("vector", "s_zmm", m + 1)
            P.do("vector", lambda e: e.reciprocal(shr.ap()[:, 0:512],
                                                  ps_a.ap()[:, 0:512]),
                 inc="s_recip")
            for dt in range(NDC):
                psb = ps_b if dt % 2 == 0 else ps_a
                if dt == 0:
                    if m == 0:
                        P.wait("tensor", "s_xc", 128)
                    P.wait("tensor", "s_bigevac", 8 * m)
                elif dt == 1:
                    P.wait("tensor", "s_recip", m + 1)
                    P.wait("tensor", "s_bigevac", 8 * m)
                else:
                    P.wait("tensor", "s_bigevac", 8 * m + dt - 1)
                for uc in range(NUC):
                    xcol = uc * D + dt * 128
                    inc = None
                    if uc == NUC - 1:
                        inc = "s_bigdc"
                    elif dt == NDC - 1 and uc == 15:
                        inc = "s_bighalf"
                    P.do("tensor", lambda e, o=psb.ap()[:, 0:512],
                         l=x_tok.ap()[:, xcol:xcol + 128],
                         r=E.ap()[:, uc * 512:(uc + 1) * 512],
                         st=(uc == 0), sp=(uc == NUC - 1):
                         e.matmul(o, l, r, start=st, stop=sp), inc=inc)
                dv = P.cnt["s_bigdc"]
                P.wait("vector", "s_bigdc", dv)
                if dt == 0:
                    P.wait("vector", "s_recip", m + 1)
                P.do("vector", lambda e, o=pT.ap()[:, dt * 512:(dt + 1) * 512],
                     i=(ps_b if dt % 2 == 0 else ps_a).ap()[:, 0:512]:
                     e.tensor_tensor(o, i, shr.ap()[:, 0:512], ALU.mult),
                     inc="s_bigevac")
            did_xq = False
            if not last_in_grp and h == 7:
                emit_xq_proj(Bg + 1, 8 * m)
                did_xq = True

            def emit_vo_group(tt, eh):
                vg = m * 8 + tt * 2 + eh
                if tt == 0 and eh == 0:
                    P.wait("tensor", "s_bigevac", 8 * (m + 1))
                    P.wait("tensor", "s_vodma", 16 * (m + 1))
                if vg >= 2:
                    P.wait("tensor", "s_voevac", vg - 1)
                if did_xq and tt == 0:
                    P.wait("tensor", "s_xqe", 4 * (Bg + 2))
                for dc in range(NDC):
                    pcol = dc * 512 + tt * 128
                    vcol = dc * D + eh * 512
                    P.do("tensor", lambda e,
                         o=ps_vo.ap()[:, eh * 512:eh * 512 + 512],
                         l=pT.ap()[:, pcol:pcol + 128],
                         r=vo_sb.ap()[:, vcol:vcol + 512],
                         st=(dc == 0), sp=(dc == NDC - 1):
                         e.matmul(o, l, r, start=st, stop=sp),
                         inc="s_vogrp" if dc == NDC - 1 else None)
                vv = P.cnt["s_vogrp"]
                P.wait("vector", "s_vogrp", vv)
                if vg >= 1:
                    P.wait("vector", "s_voevac", vg)
                if h == 0:
                    P.wait("vector", "s_attspill", 16 * Bg)
                    if g == 1:
                        P.wait("vector", "s_attpre", 16 * (b + 1))
                acol = tt * 1024 + eh * 512
                if h == 0 and g == 0:
                    P.do("vector", lambda e, o=att.ap()[:, acol:acol + 512],
                         i=ps_vo.ap()[:, eh * 512:eh * 512 + 512]:
                         e.tensor_copy(o, i), inc="s_voevac")
                else:
                    P.do("vector", lambda e, o=att.ap()[:, acol:acol + 512],
                         i=ps_vo.ap()[:, eh * 512:eh * 512 + 512]:
                         e.tensor_tensor(o, i, o, ALU.add), inc="s_voevac")

            if not last_in_grp:
                sc = emit_scores_gen(m + 1)
            else:
                sc = None
            if sc is not None:
                for _ in range(4):
                    next(sc, None)
            for tt in range(NTT):
                for eh in range(2):
                    emit_vo_group(tt, eh)
                    if sc is not None:
                        for _ in range(4):
                            next(sc, None)
            if sc is not None:
                for _ in sc:
                    pass

        def emit_vo_dma(m):
            g, h = m // 32, m % 8
            hid = g * HPG + h
            if m > 0:
                P.wait("gpsimd", "s_vogrp", 8 * m)
            src = vo_in[hid, :, :].rearrange("(dc p) e -> p dc e", p=128)
            dst = vo_sb.ap().rearrange("p (dc e) -> p dc e", dc=NDC)
            P.do("gpsimd", lambda e, d=dst, s=src: e.dma_start(d, s),
                 inc="s_vodma", k=16)

        def emit_att_preload(b):
            P.wait("sync", "s_attspill", 16 * (NB + b))
            src = ats[b * 512:(b + 1) * 512, :].rearrange("(tt p) e -> p tt e", p=128)
            dst = att.ap().rearrange("p (tt e) -> p tt e", tt=NTT)
            P.do("sync", lambda e, d=dst, s=src: e.dma_start(d, s),
                 inc="s_attpre", k=16)

        def emit_att_spill(m):
            b = (m // 8) % 4
            P.wait("sync", "s_voevac", 8 * (m + 1))
            dst = ats[b * 512:(b + 1) * 512, :].rearrange("(tt p) e -> p tt e", p=128)
            src = att.ap().rearrange("p (tt e) -> p tt e", tt=NTT)
            P.do("sync", lambda e, d=dst, s=src: e.dma_start(d, s),
                 inc="s_attspill", k=16)

        for m in range(NM):
            g, b, h = m // 32, (m // 8) % 4, m % 8
            if h == 0 and b == 0:
                emit_qk_prep(g)
                emit_xk_proj(g)
                emit_fmq_load(g * NB)
                emit_xq_proj(g * NB, 8 * 32 * g)
                emit_vo_dma(m)
                emit_scores(m)
            if h == 0 and g == 1:
                emit_att_preload(b)
            emit_exp(m)
            last = (m == NM - 1)
            last_in_grp = (m % 32 == 31)
            emit_head(m, last_in_grp, last)
            if not last and (m + 1) % 32 != 0:
                emit_vo_dma(m + 1)
                if h == 7:
                    emit_fmq_load((m + 1) // 8)
            if h == 7:
                emit_att_spill(m)

        blockA = nc.Block()
        with blockA as blk:
            plan.emit(blk)

    with contextlib.ExitStack() as stB:
        sb = lambda name, shape, dt: stB.enter_context(nc.sbuf_tensor(name, shape, dt))
        arow = sb("arow", [128, NTT * D], F32)
        xa = sb("xa", [128, 2 * NTT * D], F32)
        h1 = sb("h1", [128, 2 * NTT * D], F32)
        h1b = sb("h1b", [128, NTT * D], BF)
        h1fm = sb("h1fm", [128, 2 * NDC * TB], BF)
        junkb = sb("junkb", [128, 2 * NTT * 1024], BF)
        U1sb = sb("U1sb", [128, NDC * RK], BF)
        V1sb = sb("V1sb", [128, 2 * DFF], BF)
        U2sb = sb("U2sb", [128, NFT * RK], BF)
        V2sb = sb("V2sb", [128, 2 * D], BF)
        b1sb = sb("b1sb", [128, NFT], F32)
        b2row = sb("b2row", [128, D], F32)
        Hc = [sb(f"Hc{i}", [128, 512], BF) for i in range(4)]
        p1sb = sb("p1sb", [128, 1024], BF)
        q2sb = sb("q2sb", [128, 1024], BF)
        outb = sb("outb", [128, NTT * D], F32)
        xfb = sb("xfb", [128, NTT * D], F32)
        st4 = sb("st4", [128, 96], F32)
        csB = sb("csB", [128, 2], F32)
        onesf = sb("onesf", [128, 128], F32)
        idbfB = sb("idbfB", [128, 128], BF)

        ps_h = [stB.enter_context(nc.psum_tensor(f"ps_h{i}", [128, 512], F32))
                for i in range(2)]
        ps_q2 = [stB.enter_context(nc.psum_tensor(f"ps_q2{i}", [128, 512], F32))
                 for i in range(2)]
        ps_o2 = [stB.enter_context(nc.psum_tensor(f"ps_o2{i}", [128, 512], F32))
                 for i in range(2)]
        ps_tb = [stB.enter_context(nc.psum_tensor(f"ps_tb{i}", [128, 1024], BF))
                 for i in range(2)]

        def pcol(b, c):
            return (b % 2) * (NTT * D) + c

        def xab(b, c):
            return xa.ap()[:, pcol(b, c.start):pcol(b, c.stop)]

        def h1v(b, c):
            return h1.ap()[:, pcol(b, c.start):pcol(b, c.stop)]

        def jkb1(c):
            return junkb[:, c.start:c.stop]

        def jkb3(c):
            return junkb[:, NTT * D + c.start:NTT * D + c.stop]

        def outv(c):
            return outb.ap()[:, c.start:c.stop]

        def xfbv(c):
            return xfb.ap()[:, c.start:c.stop]

        def h1fmv(b, dc):
            o = (b % 2) * (NDC * TB) + dc * 512
            return h1fm.ap()[:, o:o + 512]

        stb = lambda b, a, z: st4.ap()[:, (b % 4) * 24 + a:(b % 4) * 24 + z]
        s1c = lambda b, tt: stb(b, tt, tt + 1)
        s2c = lambda b, tt: stb(b, 4 + tt, 5 + tt)
        m4c = lambda b, tt: stb(b, 8 + tt, 9 + tt)
        r4c = lambda b, tt: stb(b, 16 + tt, 17 + tt)

        GATE = 16 * NG * NB

        P.wait("gpsimd", "s_attspill", GATE)
        P.do("gpsimd", lambda e: e.memset(csB[:, 0:1], EPS), inc="s_jj")
        P.do("gpsimd", lambda e: e.memset(csB[:, 1:2], -0.5), inc="s_jj")
        P.do("gpsimd", lambda e: e.memset(onesf[:, :], 1.0), inc="s_jj")
        P.do("gpsimd", lambda e: e.memset(idbfB[:, :], 1.0), inc="s_jj")
        P.wait("gpsimd", "s_jj", 4)
        P.do("gpsimd", lambda e: e.affine_select(
            idbfB[:, :], idbfB[:, :], [[1, 128]], ALU.is_equal, 0.0,
            base=0, channel_multiplier=-1), inc="s_idB")
        P.do("gpsimd", lambda e: e.dma_start(b2row.ap()[0:1, :], b2_in.rearrange(
            "(o d) -> o d", o=1)), inc="s_b2l", k=16)
        P.wait("tensor", "s_attspill", GATE)
        P.wait("tensor", "s_b2l", 16)
        P.wait("tensor", "s_jj", 4)
        for half in range(2):
            P.do("tensor", lambda e, half=half: e.matmul(
                ps_o2[half].ap()[:, 0:512], onesf[0:1, :],
                b2row.ap()[0:1, half * 512:half * 512 + 512],
                start=True, stop=True), inc="s_b2m")
        P.wait("vector", "s_b2m", 2)
        for half in range(2):
            P.do("vector", lambda e, half=half: e.tensor_copy(
                b2row.ap()[:, half * 512:half * 512 + 512],
                ps_o2[half].ap()[:, 0:512]), inc="s_id2")

        def wload(sem, dst, src, slow=False):
            P.do("gpsimd", lambda e, d=dst, s=src: e.dma_start(
                d, s, allow_slow_non_contiguous=slow), inc=sem, k=16)
        wload("s_wu1", U1sb.ap().rearrange("p (dc r) -> p dc r", dc=NDC),
              u1_in.rearrange("(dc p) r -> p dc r", p=128))
        wload("s_wv1", V1sb.ap().rearrange("p (rc f) -> p rc f", rc=2),
              v1_in.rearrange("(rc p) f -> p rc f", p=128))
        wload("s_wu2", U2sb.ap().rearrange("p (fc r) -> p fc r", fc=NFT),
              u2_in.rearrange("(fc p) r -> p fc r", p=128))
        wload("s_wb1", b1sb.ap(), b1_in.rearrange("(ft p) -> p ft", p=128),
              slow=True)
        wload("s_wv2", V2sb.ap().rearrange("p (rc d) -> p rc d", rc=2),
              v2_in.rearrange("(rc p) d -> p rc d", p=128))

        def ln_stats_vec(phase, b):
            tag = f"s_var{phase}"
            P.do("vector", lambda e, b=b: e.tensor_scalar_mul(
                stb(b, 8, 12), stb(b, 0, 4), 1.0 / D), inc="s_lns")
            P.do("vector", lambda e, b=b: e.tensor_scalar_mul(
                stb(b, 12, 16), stb(b, 4, 8), 1.0 / D), inc="s_lns")
            P.wait("vector", "s_lns", P.cnt["s_lns"])
            P.do("vector", lambda e, b=b: e.tensor_mul(
                stb(b, 16, 20), stb(b, 8, 12), stb(b, 8, 12)), inc="s_lns")
            P.wait("vector", "s_lns", P.cnt["s_lns"])
            P.do("vector", lambda e, b=b: e.tensor_sub(
                stb(b, 20, 24), stb(b, 12, 16), stb(b, 16, 20)), inc=tag)

        def ln_stats_scl(phase, b):
            tag = f"s_var{phase}"
            P.wait("scalar", tag, P.cnt[tag])
            P.wait("scalar", "s_id2", 2)
            P.do("scalar", lambda e, b=b: e.activation(
                stb(b, 12, 16), stb(b, 20, 24), ACT.Ln,
                bias=csB.ap()[:, 0:1]), inc="s_lnx")
            P.wait("scalar", "s_lnx", P.cnt["s_lnx"])
            P.do("scalar", lambda e, b=b: e.activation(
                stb(b, 16, 20), stb(b, 12, 16), ACT.Exp,
                scale=csB.ap()[:, 1:2]), inc=f"s_rstd{phase}")

        def emit_B1_loads(b):
            rows = slice(b * 512, (b + 1) * 512)
            if b == 0:
                P.wait("sync", "s_attspill", GATE)
            if b >= 2:
                P.wait("sync", "s_h1a", 4 * (b - 1))
            P.do("sync", lambda e, b=b,
                 d=xa.ap()[:, (b % 2) * 4096:(b % 2) * 4096 + 4096].rearrange(
                     "p (tt d) -> p tt d", tt=NTT),
                 s=x[rows, :].rearrange("(tt p) d -> p tt d", p=128):
                 e.dma_start(d, s), inc=f"s_xr{b % 2}", k=16)
            if b >= 1:
                P.wait("scalar", "s_xa", 4 * b)
            P.do("scalar", lambda e,
                 d=arow.ap().rearrange("p (tt e) -> p tt e", tt=NTT),
                 s=ats[rows, :].rearrange("(tt p) e -> p tt e", p=128):
                 e.dma_start(d, s), inc="s_ar", k=16)

        def b1_stt(b, tts):
            if tts[0] == 0:
                if b == 0:
                    P.wait("vector", "s_attspill", GATE)
                    P.wait("scalar", "s_attspill", GATE)
                P.wait("vector", f"s_xr{b % 2}", 16 * (b // 2 + 1))
                P.wait("vector", "s_ar", 16 * (b + 1))
                if b >= 2:
                    P.wait("vector", "s_h1a", 4 * (b - 1))
            for tt in tts:
                c = slice(tt * 1024, (tt + 1) * 1024)
                P.do("vector", lambda e, c=c, tt=tt, b=b: e.scalar_tensor_tensor(
                    xab(b, c), arow.ap()[:, c], 1.0, xab(b, c),
                    ALU.mult, ALU.add, accum_out=s1c(b, tt)), inc="s_xa")

        def b1_sq(b):
            if b >= 1:
                P.wait("vector", "s_sq1", 4 * b)
            for tt in range(NTT):
                c = slice(tt * 1024, (tt + 1) * 1024)
                P.wait("vector", "s_xa", 4 * b + tt + 1)
                P.do("vector", lambda e, c=c, tt=tt, b=b: e.scalar_tensor_tensor(
                    jkb1(c), xab(b, c), 1.0, xab(b, c),
                    ALU.mult, ALU.mult, accum_out=s2c(b, tt)), inc="s_sq1")
            P.wait("vector", "s_sq1", 4 * (b + 1))
            ln_stats_vec(1, b)

        def b1_h1b(b):
            P.wait("vector", "s_rstd1", b + 1)
            if b >= 1:
                P.wait("vector", "s_htm", 32 * b)
            for tt in range(NTT):
                c = slice(tt * 1024, (tt + 1) * 1024)
                P.do("vector", lambda e, c=c, tt=tt, b=b: e.tensor_scalar(
                    h1b.ap()[:, c], xab(b, c), m4c(b, tt), r4c(b, tt),
                    ALU.subtract, ALU.mult), inc="s_h1")

        def b1_h1(b):
            if b >= 2:
                P.wait("vector", "s_tev", 8 * (b - 1))
            for tt in range(NTT):
                c = slice(tt * 1024, (tt + 1) * 1024)
                P.do("vector", lambda e, c=c, tt=tt, b=b: e.tensor_scalar(
                    h1v(b, c), xab(b, c), m4c(b, tt), r4c(b, tt),
                    ALU.subtract, ALU.mult), inc="s_h1a")
            P.wait("vector", "s_h1a", 4 * (b + 1))
            P.wait("vector", "s_id2", 2)
            for tt in range(NTT):
                c = slice(tt * 1024, (tt + 1) * 1024)
                P.do("vector", lambda e, c=c, b=b: e.tensor_tensor(
                    h1v(b, c), b2row.ap()[:, 0:1024],
                    h1v(b, c), ALU.add), inc="s_h1b2")
        def emit_B1T(b):
            P.wait("tensor", "s_idB", 1)
            for dc in range(NDC):
                for tt in range(NTT):
                    ti = dc * NTT + tt
                    pst = ps_tb[ti % 2]
                    P.wait("tensor", "s_h1", 4 * b + tt + 1)
                    if 32 * b + ti >= 2:
                        P.wait("tensor", "s_hfe", 32 * b + ti - 1)
                    hcol = tt * 1024 + dc * 128
                    P.do("tensor", lambda e, pst=pst, hcol=hcol: e.transpose(
                        pst.ap()[:, 0:128], h1b.ap()[:, hcol:hcol + 128],
                        idbfB[:, :]), inc="s_htm")
                    tv = P.cnt["s_htm"]
                    P.wait("vector", "s_htm", tv)
                    if ti == 0 and b >= 2:
                        P.wait("vector", "s_p1", 2 * (b - 1))
                    fcol = (b % 2) * (NDC * TB) + dc * 512 + tt * 128
                    P.do("vector", lambda e, pst=pst, fcol=fcol: e.tensor_copy(
                        h1fm.ap()[:, fcol:fcol + 128], pst.ap()[:, 0:128]),
                        inc="s_hfe")

        def emit_B2p1(b):
            if b == 0:
                P.wait("tensor", "s_wu1", 16)
            if b > 0:
                P.wait("tensor", "s_q2e", 2 * b)
            for rt in range(2):
                for dc in range(NDC):
                    if rt == 0:
                        P.wait("tensor", "s_hfe", 32 * b + 4 * (dc + 1))
                    col = dc * RK + rt * 128
                    P.do("tensor", lambda e, o=ps_q2[rt].ap()[:, 0:512],
                         l=U1sb.ap()[:, col:col + 128], b=b, dc=dc:
                         e.matmul(o, l, h1fmv(b, dc), start=(dc == 0),
                                  stop=(dc == NDC - 1)),
                         inc="s_p1" if dc == NDC - 1 else None)
                pv = P.cnt["s_p1"]
                P.wait("vector", "s_p1", pv)
                P.do("vector", lambda e, rt=rt: e.tensor_copy(
                    p1sb.ap()[:, rt * 512:(rt + 1) * 512],
                    ps_q2[rt].ap()[:, 0:512]), inc="s_p1e")

        def b3_o2mm(b, oi):
            tt, dh = oi // 2, oi % 2
            if oi == 0:
                P.wait("tensor", "s_q2e", 2 * (b + 1))
                if b == 0:
                    P.wait("tensor", "s_wv2", 16)
            if b == 0 and oi < 2:
                P.wait("tensor", "s_id2", 2)
            if 8 * b + oi >= 2:
                P.wait("tensor", "s_tev", 8 * b + oi - 1)
            pso = ps_o2[oi % 2]
            for r2c in range(2):
                qcol = r2c * 512 + tt * 128
                vcol = r2c * D + dh * 512
                P.do("tensor", lambda e, o=pso.ap()[:, 0:512],
                     l=q2sb.ap()[:, qcol:qcol + 128],
                     r=V2sb.ap()[:, vcol:vcol + 512],
                     st=(r2c == 0), sp=(r2c == 1):
                     e.matmul(o, l, r, start=st, stop=sp),
                     inc="s_o2m" if r2c == 1 else None)

        def b3_stats_tt(b, tt):
            c = slice(tt * 1024, (tt + 1) * 1024)
            P.wait("vector", "s_tev", 8 * b + 2 * tt + 2)
            P.do("vector", lambda e, c=c, tt=tt, b=b: e.reduce_sum(
                s1c(b, tt), xfbv(c), axis=mybir.AxisListType.X), inc="s_r2")
            if tt == 0 and b >= 1:
                P.wait("vector", "s_sq2", 4 * b)
            P.do("vector", lambda e, c=c, tt=tt, b=b: e.scalar_tensor_tensor(
                jkb3(c), xfbv(c), 1.0, xfbv(c),
                ALU.mult, ALU.mult, accum_out=s2c(b, tt)), inc="s_sq2")

        def b3_evac(b, oi, defer=False):
            tt, dh = oi // 2, oi % 2
            P.wait("vector", "s_o2m", 8 * b + oi + 1)
            if oi == 0:
                P.wait("vector", "s_h1b2", 4 * (b + 1))
                if b >= 1:
                    P.wait("vector", "s_ln2", 4 * b)
            xc = tt * 1024 + dh * 512
            P.do("vector", lambda e, b=b, oi=oi, xc=xc: e.scalar_tensor_tensor(
                xfb.ap()[:, xc:xc + 512], ps_o2[oi % 2].ap()[:, 0:512], 1.0,
                h1.ap()[:, (b % 2) * 4096 + xc:(b % 2) * 4096 + xc + 512],
                ALU.mult, ALU.add), inc="s_tev")
            if dh == 1 and not defer:
                b3_stats_tt(b, tt)

        def b3_ln2stats(b):
            P.wait("vector", "s_r2", 4 * (b + 1))
            P.wait("vector", "s_sq2", 4 * (b + 1))
            ln_stats_vec(2, b)

        def b3_ln2apply(b):
            P.wait("vector", "s_rstd2", b + 1)
            if b >= 1:
                P.wait("vector", "s_y", 16 * b)
            for tt in range(NTT):
                c = slice(tt * 1024, (tt + 1) * 1024)
                P.do("vector", lambda e, c=c, tt=tt, b=b: e.tensor_scalar(
                    outv(c), xfbv(c), m4c(b, tt), r4c(b, tt),
                    ALU.subtract, ALU.mult), inc="s_ln2")

        def b3_store(b):
            rows = slice(b * 512, (b + 1) * 512)
            P.wait("gpsimd", "s_ln2", 4 * (b + 1))
            P.do("gpsimd", lambda e,
                 d=y[rows, :].rearrange("(tt p) d -> p tt d", p=128),
                 s=outb.ap().rearrange("p (tt d) -> p tt d", tt=NTT):
                 e.dma_start(d, s), inc="s_y", k=16)

        def emit_B23(b):
            prev, nxt = b - 1, b + 1
            if prev >= 0:
                b3_evac(prev, 0)
                b3_evac(prev, 1)
            if nxt < NB:
                b1_stt(nxt, [0, 1])
            if prev >= 0:
                b3_evac(prev, 2)
                b3_evac(prev, 3)
            if nxt < NB:
                b1_stt(nxt, [2, 3])
                b1_sq(nxt)
            if prev >= 0:
                b3_evac(prev, 4)
            if nxt < NB:
                b1_h1b(nxt)
            if prev >= 0:
                b3_evac(prev, 5)
                b3_evac(prev, 6)
                b3_evac(prev, 7)
                b3_ln2stats(prev)
            for ft in range(NFT):
                hb = ps_h[ft % 2]
                hc = Hc[ft % 4]
                if ft == 0:
                    P.wait("tensor", "s_p1e", 2 * (b + 1))
                    if b == 0:
                        P.wait("tensor", "s_wv1", 16)
                        P.wait("tensor", "s_wu2", 16)
                        P.wait("scalar", "s_wb1", 16)
                if ft >= 2:
                    P.wait("tensor", "s_gelu", 32 * b + ft - 1)
                for rc in range(2):
                    col = rc * DFF + ft * 128
                    P.do("tensor", lambda e, o=hb.ap()[:, 0:512],
                         l=V1sb.ap()[:, col:col + 128],
                         r=p1sb.ap()[:, rc * 512:(rc + 1) * 512],
                         st=(rc == 0), sp=(rc == 1):
                         e.matmul(o, l, r, start=st, stop=sp),
                         inc="s_hmm" if rc == 1 else None)
                hv = P.cnt["s_hmm"]
                if ft == 24 and nxt < NB:
                    ln_stats_scl(1, nxt)
                P.wait("scalar", "s_hmm", hv)
                if ft >= 4 or b >= 1:
                    P.wait("scalar", "s_q2m", 64 * b + 2 * ft - 6)
                P.do("scalar", lambda e, hc=hc, hb=hb, ft=ft: e.activation(
                    hc[:, :], hb.ap()[:, 0:512], ACT.Gelu_apprx_tanh,
                    bias=b1sb.ap()[:, ft:ft + 1]), inc="s_gelu")
                if prev >= 0 and ft % 4 == 2:
                    b3_o2mm(prev, ft // 4)
                if ft >= 1:
                    qft = ft - 1
                    P.wait("tensor", "s_gelu", 32 * b + qft + 1)
                    for r2t in range(2):
                        col = qft * RK + r2t * 128
                        P.do("tensor", lambda e, o=ps_q2[r2t].ap()[:, 0:512],
                             l=U2sb.ap()[:, col:col + 128], r=Hc[qft % 4][:, :],
                             st=(qft == 0):
                             e.matmul(o, l, r, start=st, stop=False),
                             inc="s_q2m")
            P.wait("tensor", "s_gelu", 32 * b + 32)
            for r2t in range(2):
                col = 31 * RK + r2t * 128
                P.do("tensor", lambda e, o=ps_q2[r2t].ap()[:, 0:512],
                     l=U2sb.ap()[:, col:col + 128], r=Hc[31 % 4][:, :]:
                     e.matmul(o, l, r, start=False, stop=True),
                     inc="s_q2m")
            if prev >= 0:
                ln_stats_scl(2, prev)

        def emit_B23_tail(b):
            prev, nxt = b - 1, b + 1
            P.wait("vector", "s_q2m", 64 * (b + 1))
            if prev >= 0:
                P.wait("vector", "s_o2m", 8 * b)
            for r2t in range(2):
                P.do("vector", lambda e, r2t=r2t: e.tensor_copy(
                    q2sb.ap()[:, r2t * 512:(r2t + 1) * 512],
                    ps_q2[r2t].ap()[:, 0:512]), inc="s_q2e")
            if prev >= 0:
                b3_ln2apply(prev)
            if nxt < NB:
                b1_h1(nxt)
            if prev >= 0:
                b3_store(prev)

        def emit_B3tail():
            b = NB - 1
            for oi in range(8):
                b3_o2mm(b, oi)
            for oi in range(8):
                b3_evac(b, oi, defer=True)
            for tt in range(NTT):
                b3_stats_tt(b, tt)
            b3_ln2stats(b)
            b3_ln2apply(b)
            ln_stats_scl(2, b)
            b3_store(b)

        emit_B1_loads(0)
        b1_stt(0, [0, 1, 2, 3])
        b1_sq(0)
        ln_stats_scl(1, 0)
        b1_h1b(0)
        emit_B1T(0)
        b1_h1(0)
        for b in range(NB):
            emit_B2p1(b)
            if b + 1 < NB:
                emit_B1_loads(b + 1)
            emit_B23(b)
            if b + 1 < NB:
                emit_B1T(b + 1)
            emit_B23_tail(b)
        emit_B3tail()

        P.wait("sync", "s_y", 16 * NB)

        blockB = nc.Block()
        with blockB as blk:
            plan.emit(blk)

    return nc


_cache = {}


def kernel(L0_attn_x, L0_attn_Q, L0_attn_K, L0_attn_VO,
           L0_ffn_U1, L0_ffn_V1, L0_ffn_b1, L0_ffn_U2, L0_ffn_V2, L0_ffn_b2):
    xs = np.ascontiguousarray(np.asarray(L0_attn_x, dtype=np.float32))
    Q = np.ascontiguousarray(np.asarray(L0_attn_Q, dtype=np.float32))
    K = np.ascontiguousarray(np.asarray(L0_attn_K, dtype=np.float32))
    VO = np.ascontiguousarray(np.asarray(L0_attn_VO, dtype=np.float32))
    U1 = np.ascontiguousarray(np.asarray(L0_ffn_U1, dtype=np.float32))
    V1 = np.ascontiguousarray(np.asarray(L0_ffn_V1, dtype=np.float32))
    b1 = np.ascontiguousarray(np.asarray(L0_ffn_b1, dtype=np.float32))
    U2 = np.ascontiguousarray(np.asarray(L0_ffn_U2, dtype=np.float32))
    V2 = np.ascontiguousarray(np.asarray(L0_ffn_V2, dtype=np.float32))
    b2 = np.ascontiguousarray(np.asarray(L0_ffn_b2, dtype=np.float32))

    if "nc" not in _cache:
        _cache["nc"] = build_nc()
    nc = _cache["nc"]

    in_maps = []
    for c in range(8):
        bidx, off = c // 2, (c % 2) * TQ
        xb = np.ascontiguousarray(
            np.concatenate([xs[bidx, off:], xs[bidx, :off]], axis=0))
        in_maps.append({
            "x": xb, "q": Q, "k": K, "vo": VO,
            "u1": U1, "v1": V1, "b1": b1, "u2": U2, "v2": V2, "b2": b2,
        })
    res = run_bass_kernel_spmd(nc, in_maps, core_ids=list(range(8)))
    out = np.empty((4, T, D), dtype=np.float32)
    for c in range(8):
        bidx, off = c // 2, (c % 2) * TQ
        out[bidx, off:off + TQ] = res.results[c]["y"]
    return out


# version 20 (final)
# speedup vs baseline: 1.0843x; 1.0790x over previous
import contextlib
import numpy as np

import concourse.bass as bass
import concourse.mybir as mybir
from concourse.bass_utils import run_bass_kernel_spmd

F32 = mybir.dt.float32
BF = mybir.dt.bfloat16
ACT = mybir.ActivationFunctionType
ALU = mybir.AluOpType

D = 1024
T = 4096
TQ = 2048
NH = 16
R = 64
HPG = 8
NG = 2
NPR = HPG // 2
NB = 4
TB = 512
NTT = TB // 128
NUC = T // 128
NDC = D // 128
DFF = 4096
RK = 256
NFT = DFF // 128
EPS = 1e-5
QK_SCALE = 1.0 / 32.0
NM = NG * NB * HPG


class Plan:

    ENGINES = ("tensor", "scalar", "vector", "sync", "gpsimd")

    def __init__(self, nc):
        self.nc = nc
        self.ops = {e: [] for e in self.ENGINES}
        self.sems = {}
        self.cnt = {}

    def sem(self, name):
        if name not in self.sems:
            self.sems[name] = self.nc.alloc_semaphore(name)
            self.cnt[name] = 0
        return self.sems[name]

    def wait(self, eng, name, val):
        if val <= 0:
            return
        h = self.sem(name)
        self.ops[eng].append(lambda e, h=h, v=val: e.wait_ge(h, v))

    def do(self, eng, f, inc=None, k=1):
        if inc is None:
            self.ops[eng].append(f)
            return None
        h = self.sem(inc)
        self.cnt[inc] += k
        self.ops[eng].append(lambda e, f=f, h=h, k=k: f(e).then_inc(h, k))
        return self.cnt[inc]

    def nop_inc(self, eng, inc, k=1):
        h = self.sem(inc)
        self.cnt[inc] += k
        self.ops[eng].append(lambda e, h=h, k=k: e.nop().then_inc(h, k))
        return self.cnt[inc]

    def emit(self, block):
        secs = {
            "tensor": block.tensor, "scalar": block.scalar,
            "vector": block.vector, "sync": block.sync, "gpsimd": block.gpsimd,
        }
        for name in self.ENGINES:
            ops = self.ops[name]

            def runner(e, ops=ops):
                for f in ops:
                    f(e)
            secs[name](runner)
        self.ops = {e: [] for e in self.ENGINES}


def build_nc():
    nc = bass.Bass("TRN2", target_bir_lowering=False)

    x = nc.dram_tensor("x", [T, D], F32, kind="ExternalInput").ap()
    q_in = nc.dram_tensor("q", [NH, R, D], F32, kind="ExternalInput").ap()
    k_in = nc.dram_tensor("k", [NH, R, D], F32, kind="ExternalInput").ap()
    vo_in = nc.dram_tensor("vo", [NH, D, D], F32, kind="ExternalInput").ap()
    u1_in = nc.dram_tensor("u1", [D, RK], F32, kind="ExternalInput").ap()
    v1_in = nc.dram_tensor("v1", [RK, DFF], F32, kind="ExternalInput").ap()
    b1_in = nc.dram_tensor("b1", [DFF], F32, kind="ExternalInput").ap()
    u2_in = nc.dram_tensor("u2", [DFF, RK], F32, kind="ExternalInput").ap()
    v2_in = nc.dram_tensor("v2", [RK, D], F32, kind="ExternalInput").ap()
    b2_in = nc.dram_tensor("b2", [D], F32, kind="ExternalInput").ap()

    xts = nc.dram_tensor("xts", [T, D], BF, kind="Internal").ap()
    xfs = nc.dram_tensor("xfs", [D, T], BF, kind="Internal").ap()
    ats = nc.dram_tensor("ats", [TQ, D], F32, kind="Internal").ap()
    h1s = nc.dram_tensor("h1s", [TQ, D], BF, kind="Internal").ap()

    y = nc.dram_tensor("y", [TQ, D], F32, kind="ExternalOutput").ap()

    plan = Plan(nc)
    P = plan

    with contextlib.ExitStack() as stA:
        sb = lambda name, shape, dt: stA.enter_context(nc.sbuf_tensor(name, shape, dt))
        x_tok = sb("x_tok", [128, NUC * D], BF)
        fmc0 = sb("fmc0", [128, NDC * TB], BF)
        fmc1 = sb("fmc1", [128, NDC * TB], BF)
        xk_g = sb("xk_g", [128, NPR * T], BF)
        qT = sb("qT", [128, NPR * D], BF)
        att = sb("att", [128, NTT * D], F32)
        E = sb("E", [128, NUC * TB], BF)
        pT = sb("pT", [128, NDC * TB], BF)
        vo_sb = sb("vo_sb", [128, NDC * D], BF)
        xq = sb("xq", [128, NPR * TB], BF)
        shr = sb("shr", [128, 512], F32)
        stg0 = sb("stg0", [128, 1024], F32)
        stg1 = sb("stg1", [128, 1024], F32)
        onesb = sb("onesb", [128, 128], BF)
        idf = sb("idf", [128, 128], F32)
        csA = sb("csA", [128, 1], F32)

        kT = E

        ps_sc = stA.enter_context(nc.psum_tensor("ps_sc", [128, 2048], F32))
        ps_a = stA.enter_context(nc.psum_tensor("ps_a", [128, 512], F32))
        ps_b = stA.enter_context(nc.psum_tensor("ps_b", [128, 512], F32))
        ps_vo = stA.enter_context(nc.psum_tensor("ps_vo", [128, 1024], F32))

        P.do("gpsimd", lambda e: e.memset(onesb[:, :], 1.0), inc="s_ii")
        P.do("gpsimd", lambda e: e.memset(idf[:, :], 1.0), inc="s_ii")
        P.do("gpsimd", lambda e: e.memset(csA[:, :], QK_SCALE), inc="s_ii")
        P.wait("gpsimd", "s_ii", 3)
        P.do("gpsimd", lambda e: e.affine_select(
            idf[:, :], idf[:, :], [[1, 128]], ALU.is_equal, 0.0,
            base=0, channel_multiplier=-1), inc="s_id")

        for j in range(4):
            P.do("gpsimd", lambda e, d=xts[j * 1024:(j + 1) * 1024, :],
                 s=x[j * 1024:(j + 1) * 1024, :]: e.dma_start(d, s),
                 inc=f"s_xtw{j}", k=16)
            P.wait("gpsimd", f"s_xtw{j}", 16)
        for i in range(8):
            rows = x[i * 512:(i + 1) * 512, :].rearrange("(uc p) d -> p uc d", p=128)
            dst = x_tok.ap()[:, i * 4096:(i + 1) * 4096].rearrange(
                "p (uc d) -> p uc d", uc=4)
            P.do("gpsimd", lambda e, d=dst, s=rows: e.dma_start(d, s), inc="s_xc", k=16)

        for seg in range(4):
            P.wait("scalar", f"s_xtw{seg}", 16)
            for dc in range(NDC):
                if seg > 0:
                    P.wait("scalar", f"s_fw{dc}", 16 * seg)
                P.do("scalar", lambda e,
                     d=E.ap()[:, 4096 + dc * 1024:4096 + (dc + 1) * 1024],
                     s=xts[seg * 1024:(seg + 1) * 1024, dc * 128:(dc + 1) * 128]:
                     e.dma_start_transpose(d, s), inc=f"s_fT{dc}", k=16)
            for dc in range(NDC):
                P.wait("scalar", f"s_fT{dc}", 16 * (seg + 1))
                P.do("scalar", lambda e,
                     d=xfs[dc * 128:(dc + 1) * 128, seg * 1024:(seg + 1) * 1024],
                     s=E.ap()[:, 4096 + dc * 1024:4096 + (dc + 1) * 1024]:
                     e.dma_start(d, s), inc=f"s_fw{dc}", k=16)

        def emit_qk_prep(g):
            kT_gate_done = [False]
            k_load = [NPR * 2 * g]
            for pr in range(NPR):
                for which, src_t, dst_t, scale in (
                        (0, q_in, qT, 1.0), (1, k_in, kT, QK_SCALE)):
                    kl = k_load[0]
                    k_load[0] += 1
                    stg = stg0 if kl % 2 == 0 else stg1
                    hid0 = g * HPG + 2 * pr
                    src2 = src_t[hid0:hid0 + 2, :, :].rearrange("two r d -> (two r) d")
                    if kl >= 2:
                        P.wait("sync", "s_qkT", 8 * (kl - 1))
                    P.do("sync", lambda e, d=stg.ap()[:, 0:1024], s=src2:
                         e.dma_start(d, s), inc=f"s_qkld{kl % 2}", k=16)
                    ldv = P.cnt[f"s_qkld{kl % 2}"]
                    for dc in range(NDC):
                        n_t = P.cnt.get("s_qkT", 0) + 1
                        psb = ps_a if n_t % 2 == 1 else ps_b
                        P.wait("tensor", f"s_qkld{kl % 2}", ldv)
                        if g > 0 and n_t == 64 * g + 1:
                            P.wait("tensor", "s_bigevac", 8 * HPG * NB * g)
                        P.wait("tensor", "s_qke", n_t - 2)
                        P.wait("tensor", "s_id", 1)
                        P.do("tensor", lambda e, o=psb.ap()[:, 0:128],
                             i=stg.ap()[:, dc * 128:(dc + 1) * 128]:
                             e.transpose(o, i, idf[:, :]), inc="s_qkT")
                        tv = P.cnt["s_qkT"]
                        P.wait("vector", "s_qkT", tv)
                        if which == 1 and not kT_gate_done[0]:
                            P.wait("vector", "s_bigdc", 8 * HPG * NB * g)
                            kT_gate_done[0] = True
                        col = pr * 1024 + dc * 128
                        if which == 0:
                            P.do("vector", lambda e, o=dst_t.ap()[:, col:col + 128],
                                 i=psb.ap()[:, 0:128]: e.tensor_copy(o, i),
                                 inc="s_qke")
                        else:
                            P.do("vector", lambda e, o=dst_t.ap()[:, col:col + 128],
                                 i=psb.ap()[:, 0:128]: e.tensor_scalar_mul(
                                     o, i, QK_SCALE), inc="s_qke")

        def emit_xk_proj(g):
            for uc in range(8):
                buf = fmc0 if uc % 2 == 0 else fmc1
                if g == 0:
                    for dc in range(NDC):
                        P.wait("sync", f"s_fw{dc}", 16 * (uc // 2 + 1))
                if uc >= 2:
                    P.wait("sync", "s_xkmm", 32 * g + 4 * (uc - 1))
                elif g > 0 and uc == 0:
                    P.wait("sync", "s_xkmm", 32 * g)
                elif g > 0 and uc == 1:
                    P.wait("sync", "s_xqmm", 16 * g)
                src = xfs[:, uc * 512:(uc + 1) * 512].rearrange(
                    "(dc p) t -> p dc t", p=128)
                dst = buf.ap().rearrange("p (dc t) -> p dc t", dc=NDC)
                fmsem = f"s_fmk{uc % 2}"
                P.do("sync", lambda e, d=dst, s=src: e.dma_start(d, s),
                     inc=fmsem, k=16)
                fv = P.cnt[fmsem]
                P.wait("tensor", fmsem, fv)
                P.wait("tensor", "s_qke", 64 * (g + 1))
                if 32 * g + 4 * uc > 0:
                    P.wait("tensor", "s_xke", 32 * g + 4 * uc)
                if uc == 0 and g > 0:
                    P.wait("tensor", "s_expgrp", 16 * HPG * NB * g)
                for pr in range(NPR):
                    for dc in range(NDC):
                        kcol = pr * 1024 + dc * 128
                        P.do("tensor", lambda e,
                             o=ps_sc.ap()[:, pr * 512:(pr + 1) * 512],
                             l=kT.ap()[:, kcol:kcol + 128],
                             r=buf.ap()[:, dc * 512:(dc + 1) * 512],
                             st=(dc == 0), sp=(dc == NDC - 1):
                             e.matmul(o, l, r, start=st, stop=sp),
                             inc="s_xkmm" if dc == NDC - 1 else None)
                    mv = P.cnt["s_xkmm"]
                    P.wait("vector", "s_xkmm", mv)
                    dcol = pr * T + uc * 512
                    P.do("vector", lambda e, o=xk_g.ap()[:, dcol:dcol + 512],
                         i=ps_sc.ap()[:, pr * 512:(pr + 1) * 512]:
                         e.tensor_copy(o, i), inc="s_xke")

        def emit_fmq_load(Bg):
            if Bg % NB == 0:
                P.wait("sync", "s_xkmm", 32 * (Bg // NB + 1))
            else:
                P.wait("sync", "s_xqmm", 4 * Bg)
            b = Bg % NB
            src = xfs[:, b * 512:b * 512 + 512].rearrange("(dc p) t -> p dc t", p=128)
            dst = fmc1.ap().rearrange("p (dc t) -> p dc t", dc=NDC)
            P.do("sync", lambda e, d=dst, s=src: e.dma_start(d, s), inc="s_fmq", k=16)

        def emit_xq_proj(Bg, voevac_gate):
            P.wait("tensor", "s_fmq", 16 * (Bg + 1))
            P.wait("tensor", "s_qke", 64 * (Bg // NB + 1))
            P.wait("tensor", "s_voevac", voevac_gate)
            for pr in range(NPR):
                if pr >= 2:
                    P.wait("tensor", "s_xqe", 4 * Bg + pr - 1)
                for dc in range(NDC):
                    qcol = pr * 1024 + dc * 128
                    P.do("tensor", lambda e,
                         o=ps_vo.ap()[:, (pr % 2) * 512:(pr % 2) * 512 + 512],
                         l=qT.ap()[:, qcol:qcol + 128],
                         r=fmc1.ap()[:, dc * 512:(dc + 1) * 512],
                         st=(dc == 0), sp=(dc == NDC - 1):
                         e.matmul(o, l, r, start=st, stop=sp),
                         inc="s_xqmm" if dc == NDC - 1 else None)
                mv = P.cnt["s_xqmm"]
                P.wait("vector", "s_xqmm", mv)
                P.do("vector", lambda e, o=xq.ap()[:, pr * 512:(pr + 1) * 512],
                     i=ps_vo.ap()[:, (pr % 2) * 512:(pr % 2) * 512 + 512]:
                     e.tensor_copy(o, i), inc="s_xqe")

        def emit_scores_gen(m):
            g, b, h = m // 32, (m // 8) % 4, m % 8
            pr, hh = h // 2, h % 2
            Bg = g * NB + b
            P.wait("tensor", "s_xqe", 4 * (Bg + 1))
            P.wait("tensor", "s_xke", 32 * (g + 1))
            for uj in range(NUC):
                j = uj // 2
                if uj % 2 == 0:
                    P.wait("tensor", "s_expgrp", m * 16 + j - 1)
                kcol = pr * T + uj * 128
                c0 = (uj % 4) * 512
                P.do("tensor", lambda e,
                     o=ps_sc.ap()[:, c0:c0 + 512],
                     l=xk_g.ap()[hh * 64:hh * 64 + 64, kcol:kcol + 128],
                     r=xq.ap()[hh * 64:hh * 64 + 64, pr * 512:(pr + 1) * 512]:
                     e.matmul(o, l, r, start=True, stop=True),
                     inc="s_scgrp" if uj % 2 == 1 else None)
                yield uj

        def emit_scores(m):
            for _ in emit_scores_gen(m):
                pass

        def emit_exp(m):
            g = m // 32
            for j in range(16):
                P.wait("scalar", "s_scgrp", m * 16 + j + 1)
                if j == 0:
                    P.wait("scalar", "s_bighalf", m)
                    if m % 32 == 0:
                        P.wait("scalar", "s_xkmm", 32 * (g + 1))
                elif j == 8:
                    P.wait("scalar", "s_bigdc", 8 * m)
                P.do("scalar", lambda e,
                     o=E.ap()[:, (2 * j) * 512:(2 * j + 2) * 512],
                     i=ps_sc.ap()[:, (j % 2) * 1024:(j % 2) * 1024 + 1024]:
                     e.activation(o, i, ACT.Exp), inc="s_expgrp")

        def emit_head(m, last_in_grp, last):
            g, b, h = m // 32, (m // 8) % 4, m % 8
            Bg = g * NB + b
            P.wait("tensor", "s_expgrp", 16 * (m + 1))
            P.wait("tensor", "s_bigevac", 8 * m)
            for uc in range(NUC):
                P.do("tensor", lambda e, o=ps_a.ap()[:, 0:512],
                     l=onesb[:, :], r=E.ap()[:, uc * 512:(uc + 1) * 512],
                     st=(uc == 0), sp=(uc == NUC - 1):
                     e.matmul(o, l, r, start=st, stop=sp),
                     inc="s_zmm" if uc == NUC - 1 else None)
            P.wait("vector", "s_zmm", m + 1)
            P.do("vector", lambda e: e.reciprocal(shr.ap()[:, 0:512],
                                                  ps_a.ap()[:, 0:512]),
                 inc="s_recip")
            for dt in range(NDC):
                psb = ps_b if dt % 2 == 0 else ps_a
                if dt == 0:
                    if m == 0:
                        P.wait("tensor", "s_xc", 128)
                    P.wait("tensor", "s_bigevac", 8 * m)
                elif dt == 1:
                    P.wait("tensor", "s_recip", m + 1)
                    P.wait("tensor", "s_bigevac", 8 * m)
                else:
                    P.wait("tensor", "s_bigevac", 8 * m + dt - 1)
                for uc in range(NUC):
                    xcol = uc * D + dt * 128
                    inc = None
                    if uc == NUC - 1:
                        inc = "s_bigdc"
                    elif dt == NDC - 1 and uc == 15:
                        inc = "s_bighalf"
                    P.do("tensor", lambda e, o=psb.ap()[:, 0:512],
                         l=x_tok.ap()[:, xcol:xcol + 128],
                         r=E.ap()[:, uc * 512:(uc + 1) * 512],
                         st=(uc == 0), sp=(uc == NUC - 1):
                         e.matmul(o, l, r, start=st, stop=sp), inc=inc)
                dv = P.cnt["s_bigdc"]
                P.wait("vector", "s_bigdc", dv)
                if dt == 0:
                    P.wait("vector", "s_recip", m + 1)
                P.do("vector", lambda e, o=pT.ap()[:, dt * 512:(dt + 1) * 512],
                     i=(ps_b if dt % 2 == 0 else ps_a).ap()[:, 0:512]:
                     e.tensor_tensor(o, i, shr.ap()[:, 0:512], ALU.mult),
                     inc="s_bigevac")
            did_xq = False
            if not last_in_grp and h == 7:
                emit_xq_proj(Bg + 1, 8 * m)
                did_xq = True

            def emit_vo_group(tt, eh):
                vg = m * 8 + tt * 2 + eh
                if tt == 0 and eh == 0:
                    P.wait("tensor", "s_bigevac", 8 * (m + 1))
                    P.wait("tensor", "s_vodma", 16 * (m + 1))
                if vg >= 2:
                    P.wait("tensor", "s_voevac", vg - 1)
                if did_xq and tt == 0:
                    P.wait("tensor", "s_xqe", 4 * (Bg + 2))
                for dc in range(NDC):
                    pcol = dc * 512 + tt * 128
                    vcol = dc * D + eh * 512
                    P.do("tensor", lambda e,
                         o=ps_vo.ap()[:, eh * 512:eh * 512 + 512],
                         l=pT.ap()[:, pcol:pcol + 128],
                         r=vo_sb.ap()[:, vcol:vcol + 512],
                         st=(dc == 0), sp=(dc == NDC - 1):
                         e.matmul(o, l, r, start=st, stop=sp),
                         inc="s_vogrp" if dc == NDC - 1 else None)
                vv = P.cnt["s_vogrp"]
                P.wait("vector", "s_vogrp", vv)
                if vg >= 1:
                    P.wait("vector", "s_voevac", vg)
                if h == 0:
                    P.wait("vector", "s_attspill", 16 * Bg)
                    if g == 1:
                        P.wait("vector", "s_attpre", 16 * (b + 1))
                acol = tt * 1024 + eh * 512
                if h == 0 and g == 0:
                    P.do("vector", lambda e, o=att.ap()[:, acol:acol + 512],
                         i=ps_vo.ap()[:, eh * 512:eh * 512 + 512]:
                         e.tensor_copy(o, i), inc="s_voevac")
                else:
                    P.do("vector", lambda e, o=att.ap()[:, acol:acol + 512],
                         i=ps_vo.ap()[:, eh * 512:eh * 512 + 512]:
                         e.tensor_tensor(o, i, o, ALU.add), inc="s_voevac")

            if not last_in_grp:
                sc = emit_scores_gen(m + 1)
            else:
                sc = None
            if sc is not None:
                for _ in range(4):
                    next(sc, None)
            for tt in range(NTT):
                for eh in range(2):
                    emit_vo_group(tt, eh)
                    if sc is not None:
                        for _ in range(4):
                            next(sc, None)
            if sc is not None:
                for _ in sc:
                    pass

        def emit_vo_dma(m):
            g, h = m // 32, m % 8
            hid = g * HPG + h
            if m > 0:
                P.wait("gpsimd", "s_vogrp", 8 * m)
            src = vo_in[hid, :, :].rearrange("(dc p) e -> p dc e", p=128)
            dst = vo_sb.ap().rearrange("p (dc e) -> p dc e", dc=NDC)
            P.do("gpsimd", lambda e, d=dst, s=src: e.dma_start(d, s),
                 inc="s_vodma", k=16)

        def emit_att_preload(b):
            P.wait("sync", "s_attspill", 16 * (NB + b))
            src = ats[b * 512:(b + 1) * 512, :].rearrange("(tt p) e -> p tt e", p=128)
            dst = att.ap().rearrange("p (tt e) -> p tt e", tt=NTT)
            P.do("sync", lambda e, d=dst, s=src: e.dma_start(d, s),
                 inc="s_attpre", k=16)

        def emit_att_spill(m):
            b = (m // 8) % 4
            P.wait("sync", "s_voevac", 8 * (m + 1))
            dst = ats[b * 512:(b + 1) * 512, :].rearrange("(tt p) e -> p tt e", p=128)
            src = att.ap().rearrange("p (tt e) -> p tt e", tt=NTT)
            P.do("sync", lambda e, d=dst, s=src: e.dma_start(d, s),
                 inc="s_attspill", k=16)

        for m in range(NM):
            g, b, h = m // 32, (m // 8) % 4, m % 8
            if h == 0 and b == 0:
                emit_qk_prep(g)
                emit_xk_proj(g)
                emit_fmq_load(g * NB)
                emit_xq_proj(g * NB, 8 * 32 * g)
                emit_vo_dma(m)
                emit_scores(m)
            if h == 0 and g == 1:
                emit_att_preload(b)
            emit_exp(m)
            last = (m == NM - 1)
            last_in_grp = (m % 32 == 31)
            emit_head(m, last_in_grp, last)
            if not last and (m + 1) % 32 != 0:
                emit_vo_dma(m + 1)
                if h == 7:
                    emit_fmq_load((m + 1) // 8)
            if h == 7:
                emit_att_spill(m)

        blockA = nc.Block()
        with blockA as blk:
            plan.emit(blk)

    with contextlib.ExitStack() as stB:
        sb = lambda name, shape, dt: stB.enter_context(nc.sbuf_tensor(name, shape, dt))
        arow = sb("arow", [128, NTT * D], F32)
        xa = sb("xa", [128, 2 * NTT * D], F32)
        h1 = sb("h1", [128, 2 * NTT * D], F32)
        h1b = sb("h1b", [128, NTT * D], BF)
        h1fm = sb("h1fm", [128, 2 * NDC * TB], BF)
        junkb = sb("junkb", [128, 2 * NTT * 1024], BF)
        U1sb = sb("U1sb", [128, NDC * RK], BF)
        V1sb = sb("V1sb", [128, 2 * DFF], BF)
        U2sb = sb("U2sb", [128, NFT * RK], BF)
        V2sb = sb("V2sb", [128, 2 * D], BF)
        b1sb = sb("b1sb", [128, NFT], F32)
        b2row = sb("b2row", [128, D], F32)
        Hc = [sb(f"Hc{i}", [128, 512], BF) for i in range(4)]
        p1sb = sb("p1sb", [128, 1024], BF)
        q2sb = sb("q2sb", [128, 1024], BF)
        outb = sb("outb", [128, NTT * D], F32)
        xfb = sb("xfb", [128, NTT * D], F32)
        st4 = sb("st4", [128, 96], F32)
        csB = sb("csB", [128, 2], F32)
        onesf = sb("onesf", [128, 128], F32)
        idbfB = sb("idbfB", [128, 128], BF)

        ps_h = [stB.enter_context(nc.psum_tensor(f"ps_h{i}", [128, 512], F32))
                for i in range(2)]
        ps_q2 = [stB.enter_context(nc.psum_tensor(f"ps_q2{i}", [128, 512], F32))
                 for i in range(2)]
        ps_o2 = [stB.enter_context(nc.psum_tensor(f"ps_o2{i}", [128, 512], F32))
                 for i in range(2)]
        ps_tb = [stB.enter_context(nc.psum_tensor(f"ps_tb{i}", [128, 1024], BF))
                 for i in range(2)]

        def pcol(b, c):
            return (b % 2) * (NTT * D) + c

        def xab(b, c):
            return xa.ap()[:, pcol(b, c.start):pcol(b, c.stop)]

        def h1v(b, c):
            return h1.ap()[:, pcol(b, c.start):pcol(b, c.stop)]

        def jkb1(c):
            return junkb[:, c.start:c.stop]

        def jkb3(c):
            return junkb[:, NTT * D + c.start:NTT * D + c.stop]

        def outv(c):
            return outb.ap()[:, c.start:c.stop]

        def xfbv(c):
            return xfb.ap()[:, c.start:c.stop]

        def h1fmv(b, dc):
            o = (b % 2) * (NDC * TB) + dc * 512
            return h1fm.ap()[:, o:o + 512]

        stb = lambda b, a, z: st4.ap()[:, (b % 4) * 24 + a:(b % 4) * 24 + z]
        s1c = lambda b, tt: stb(b, tt, tt + 1)
        s2c = lambda b, tt: stb(b, 4 + tt, 5 + tt)
        m4c = lambda b, tt: stb(b, 8 + tt, 9 + tt)
        r4c = lambda b, tt: stb(b, 16 + tt, 17 + tt)

        GATE = 16 * NG * NB

        P.wait("gpsimd", "s_qkT", 128)
        P.do("gpsimd", lambda e: e.memset(csB[:, 0:1], EPS), inc="s_jj")
        P.do("gpsimd", lambda e: e.memset(csB[:, 1:2], -0.5), inc="s_jj")
        P.do("gpsimd", lambda e: e.memset(onesf[:, :], 1.0), inc="s_jj")
        P.do("gpsimd", lambda e: e.memset(idbfB[:, :], 1.0), inc="s_jj")
        P.wait("gpsimd", "s_jj", 4)
        P.do("gpsimd", lambda e: e.affine_select(
            idbfB[:, :], idbfB[:, :], [[1, 128]], ALU.is_equal, 0.0,
            base=0, channel_multiplier=-1), inc="s_idB")
        P.wait("gpsimd", "s_bigdc", 512)
        P.do("gpsimd", lambda e: e.dma_start(b2row.ap()[0:1, :], b2_in.rearrange(
            "(o d) -> o d", o=1)), inc="s_b2l", k=16)
        P.wait("tensor", "s_bigevac", 8 * HPG * NB * NG)
        P.wait("tensor", "s_recip", NM)
        P.wait("tensor", "s_b2l", 16)
        P.wait("tensor", "s_jj", 4)
        for half in range(2):
            P.do("tensor", lambda e, half=half: e.matmul(
                ps_o2[half].ap()[:, 0:512], onesf[0:1, :],
                b2row.ap()[0:1, half * 512:half * 512 + 512],
                start=True, stop=True), inc="s_b2m")
        P.wait("vector", "s_b2m", 2)
        for half in range(2):
            P.do("vector", lambda e, half=half: e.tensor_copy(
                b2row.ap()[:, half * 512:half * 512 + 512],
                ps_o2[half].ap()[:, 0:512]), inc="s_id2")

        def wload(sem, dst, src, slow=False):
            P.do("gpsimd", lambda e, d=dst, s=src: e.dma_start(
                d, s, allow_slow_non_contiguous=slow), inc=sem, k=16)
        wload("s_wu2", U2sb.ap().rearrange("p (fc r) -> p fc r", fc=NFT),
              u2_in.rearrange("(fc p) r -> p fc r", p=128))
        wload("s_wb1", b1sb.ap(), b1_in.rearrange("(ft p) -> p ft", p=128),
              slow=True)
        wload("s_wv2", V2sb.ap().rearrange("p (rc d) -> p rc d", rc=2),
              v2_in.rearrange("(rc p) d -> p rc d", p=128))
        P.wait("gpsimd", "s_attspill", GATE)
        wload("s_wu1", U1sb.ap().rearrange("p (dc r) -> p dc r", dc=NDC),
              u1_in.rearrange("(dc p) r -> p dc r", p=128))
        wload("s_wv1", V1sb.ap().rearrange("p (rc f) -> p rc f", rc=2),
              v1_in.rearrange("(rc p) f -> p rc f", p=128))

        def ln_stats_vec(phase, b):
            tag = f"s_var{phase}"
            P.do("vector", lambda e, b=b: e.tensor_scalar_mul(
                stb(b, 8, 12), stb(b, 0, 4), 1.0 / D), inc="s_lns")
            P.do("vector", lambda e, b=b: e.tensor_scalar_mul(
                stb(b, 12, 16), stb(b, 4, 8), 1.0 / D), inc="s_lns")
            P.wait("vector", "s_lns", P.cnt["s_lns"])
            P.do("vector", lambda e, b=b: e.tensor_mul(
                stb(b, 16, 20), stb(b, 8, 12), stb(b, 8, 12)), inc="s_lns")
            P.wait("vector", "s_lns", P.cnt["s_lns"])
            P.do("vector", lambda e, b=b: e.tensor_sub(
                stb(b, 20, 24), stb(b, 12, 16), stb(b, 16, 20)), inc=tag)

        def ln_stats_scl(phase, b):
            tag = f"s_var{phase}"
            P.wait("scalar", tag, P.cnt[tag])
            P.wait("scalar", "s_id2", 2)
            P.do("scalar", lambda e, b=b: e.activation(
                stb(b, 12, 16), stb(b, 20, 24), ACT.Ln,
                bias=csB.ap()[:, 0:1]), inc="s_lnx")
            P.wait("scalar", "s_lnx", P.cnt["s_lnx"])
            P.do("scalar", lambda e, b=b: e.activation(
                stb(b, 16, 20), stb(b, 12, 16), ACT.Exp,
                scale=csB.ap()[:, 1:2]), inc=f"s_rstd{phase}")

        def emit_B1_loads(b):
            rows = slice(b * 512, (b + 1) * 512)
            if b == 0:
                P.wait("sync", "s_bigdc", 8 * NM)
            if b >= 2:
                P.wait("sync", "s_h1a", 4 * (b - 1))
            P.do("sync", lambda e, b=b,
                 d=xa.ap()[:, (b % 2) * 4096:(b % 2) * 4096 + 4096].rearrange(
                     "p (tt d) -> p tt d", tt=NTT),
                 s=x[rows, :].rearrange("(tt p) d -> p tt d", p=128):
                 e.dma_start(d, s), inc=f"s_xr{b % 2}", k=16)
            if b == 0:
                P.wait("scalar", "s_bigdc", 8 * NM)
                P.wait("scalar", "s_attspill", 80)
            if b >= 1:
                P.wait("scalar", "s_xa", 4 * b)
            P.do("scalar", lambda e,
                 d=arow.ap().rearrange("p (tt e) -> p tt e", tt=NTT),
                 s=ats[rows, :].rearrange("(tt p) e -> p tt e", p=128):
                 e.dma_start(d, s), inc="s_ar", k=16)

        def b1_stt(b, tts):
            if tts[0] == 0:
                P.wait("vector", f"s_xr{b % 2}", 16 * (b // 2 + 1))
                P.wait("vector", "s_ar", 16 * (b + 1))
                if b >= 2:
                    P.wait("vector", "s_h1a", 4 * (b - 1))
            for tt in tts:
                c = slice(tt * 1024, (tt + 1) * 1024)
                P.do("vector", lambda e, c=c, tt=tt, b=b: e.scalar_tensor_tensor(
                    xab(b, c), arow.ap()[:, c], 1.0, xab(b, c),
                    ALU.mult, ALU.add, accum_out=s1c(b, tt)), inc="s_xa")

        def b1_sq(b):
            if b >= 1:
                P.wait("vector", "s_sq1", 4 * b)
            for tt in range(NTT):
                c = slice(tt * 1024, (tt + 1) * 1024)
                P.wait("vector", "s_xa", 4 * b + tt + 1)
                P.do("vector", lambda e, c=c, tt=tt, b=b: e.scalar_tensor_tensor(
                    jkb1(c), xab(b, c), 1.0, xab(b, c),
                    ALU.mult, ALU.mult, accum_out=s2c(b, tt)), inc="s_sq1")
            P.wait("vector", "s_sq1", 4 * (b + 1))
            ln_stats_vec(1, b)

        def b1_h1b(b):
            P.wait("vector", "s_rstd1", b + 1)
            if b >= 1:
                P.wait("vector", "s_htm", 32 * b)
            for tt in range(NTT):
                c = slice(tt * 1024, (tt + 1) * 1024)
                P.do("vector", lambda e, c=c, tt=tt, b=b: e.tensor_scalar(
                    h1b.ap()[:, c], xab(b, c), m4c(b, tt), r4c(b, tt),
                    ALU.subtract, ALU.mult), inc="s_h1")

        def b1_h1(b):
            if b >= 2:
                P.wait("vector", "s_tev", 8 * (b - 1))
            for tt in range(NTT):
                c = slice(tt * 1024, (tt + 1) * 1024)
                P.do("vector", lambda e, c=c, tt=tt, b=b: e.tensor_scalar(
                    h1v(b, c), xab(b, c), m4c(b, tt), r4c(b, tt),
                    ALU.subtract, ALU.mult), inc="s_h1a")
            P.wait("vector", "s_h1a", 4 * (b + 1))
            P.wait("vector", "s_id2", 2)
            for tt in range(NTT):
                c = slice(tt * 1024, (tt + 1) * 1024)
                P.do("vector", lambda e, c=c, b=b: e.tensor_tensor(
                    h1v(b, c), b2row.ap()[:, 0:1024],
                    h1v(b, c), ALU.add), inc="s_h1b2")
        def emit_B1T(b):
            P.wait("tensor", "s_idB", 1)
            for dc in range(NDC):
                for tt in range(NTT):
                    ti = dc * NTT + tt
                    pst = ps_tb[ti % 2]
                    P.wait("tensor", "s_h1", 4 * b + tt + 1)
                    if 32 * b + ti >= 2:
                        P.wait("tensor", "s_hfe", 32 * b + ti - 1)
                    hcol = tt * 1024 + dc * 128
                    P.do("tensor", lambda e, pst=pst, hcol=hcol: e.transpose(
                        pst.ap()[:, 0:128], h1b.ap()[:, hcol:hcol + 128],
                        idbfB[:, :]), inc="s_htm")
                    tv = P.cnt["s_htm"]
                    P.wait("vector", "s_htm", tv)
                    if ti == 0 and b >= 2:
                        P.wait("vector", "s_p1", 2 * (b - 1))
                    fcol = (b % 2) * (NDC * TB) + dc * 512 + tt * 128
                    P.do("vector", lambda e, pst=pst, fcol=fcol: e.tensor_copy(
                        h1fm.ap()[:, fcol:fcol + 128], pst.ap()[:, 0:128]),
                        inc="s_hfe")

        def emit_B2p1(b):
            if b == 0:
                P.wait("tensor", "s_wu1", 16)
            if b > 0:
                P.wait("tensor", "s_q2e", 2 * b)
            for rt in range(2):
                for dc in range(NDC):
                    if rt == 0:
                        P.wait("tensor", "s_hfe", 32 * b + 4 * (dc + 1))
                    col = dc * RK + rt * 128
                    P.do("tensor", lambda e, o=ps_q2[rt].ap()[:, 0:512],
                         l=U1sb.ap()[:, col:col + 128], b=b, dc=dc:
                         e.matmul(o, l, h1fmv(b, dc), start=(dc == 0),
                                  stop=(dc == NDC - 1)),
                         inc="s_p1" if dc == NDC - 1 else None)
                pv = P.cnt["s_p1"]
                P.wait("vector", "s_p1", pv)
                P.do("vector", lambda e, rt=rt: e.tensor_copy(
                    p1sb.ap()[:, rt * 512:(rt + 1) * 512],
                    ps_q2[rt].ap()[:, 0:512]), inc="s_p1e")

        def b3_o2mm(b, oi):
            tt, dh = oi // 2, oi % 2
            if oi == 0:
                P.wait("tensor", "s_q2e", 2 * (b + 1))
                if b == 0:
                    P.wait("tensor", "s_wv2", 16)
            if b == 0 and oi < 2:
                P.wait("tensor", "s_id2", 2)
            if 8 * b + oi >= 2:
                P.wait("tensor", "s_tev", 8 * b + oi - 1)
            pso = ps_o2[oi % 2]
            for r2c in range(2):
                qcol = r2c * 512 + tt * 128
                vcol = r2c * D + dh * 512
                P.do("tensor", lambda e, o=pso.ap()[:, 0:512],
                     l=q2sb.ap()[:, qcol:qcol + 128],
                     r=V2sb.ap()[:, vcol:vcol + 512],
                     st=(r2c == 0), sp=(r2c == 1):
                     e.matmul(o, l, r, start=st, stop=sp),
                     inc="s_o2m" if r2c == 1 else None)

        def b3_stats_tt(b, tt):
            c = slice(tt * 1024, (tt + 1) * 1024)
            P.wait("vector", "s_tev", 8 * b + 2 * tt + 2)
            P.do("vector", lambda e, c=c, tt=tt, b=b: e.reduce_sum(
                s1c(b, tt), xfbv(c), axis=mybir.AxisListType.X), inc="s_r2")
            if tt == 0 and b >= 1:
                P.wait("vector", "s_sq2", 4 * b)
            P.do("vector", lambda e, c=c, tt=tt, b=b: e.scalar_tensor_tensor(
                jkb3(c), xfbv(c), 1.0, xfbv(c),
                ALU.mult, ALU.mult, accum_out=s2c(b, tt)), inc="s_sq2")

        def b3_evac(b, oi, defer=False):
            tt, dh = oi // 2, oi % 2
            P.wait("vector", "s_o2m", 8 * b + oi + 1)
            if oi == 0:
                P.wait("vector", "s_h1b2", 4 * (b + 1))
                if b >= 1:
                    P.wait("vector", "s_ln2", 4 * b)
            xc = tt * 1024 + dh * 512
            P.do("vector", lambda e, b=b, oi=oi, xc=xc: e.scalar_tensor_tensor(
                xfb.ap()[:, xc:xc + 512], ps_o2[oi % 2].ap()[:, 0:512], 1.0,
                h1.ap()[:, (b % 2) * 4096 + xc:(b % 2) * 4096 + xc + 512],
                ALU.mult, ALU.add), inc="s_tev")
            if dh == 1 and not defer:
                b3_stats_tt(b, tt)

        def b3_ln2stats(b):
            P.wait("vector", "s_r2", 4 * (b + 1))
            P.wait("vector", "s_sq2", 4 * (b + 1))
            ln_stats_vec(2, b)

        def b3_ln2apply(b):
            P.wait("vector", "s_rstd2", b + 1)
            if b >= 1:
                P.wait("vector", "s_y", 16 * b)
            for tt in range(NTT):
                c = slice(tt * 1024, (tt + 1) * 1024)
                P.do("vector", lambda e, c=c, tt=tt, b=b: e.tensor_scalar(
                    outv(c), xfbv(c), m4c(b, tt), r4c(b, tt),
                    ALU.subtract, ALU.mult), inc="s_ln2")

        def b3_store(b):
            rows = slice(b * 512, (b + 1) * 512)
            P.wait("gpsimd", "s_ln2", 4 * (b + 1))
            P.do("gpsimd", lambda e,
                 d=y[rows, :].rearrange("(tt p) d -> p tt d", p=128),
                 s=outb.ap().rearrange("p (tt d) -> p tt d", tt=NTT):
                 e.dma_start(d, s), inc="s_y", k=16)

        def emit_B23(b):
            prev, nxt = b - 1, b + 1
            if prev >= 0:
                b3_evac(prev, 0)
                b3_evac(prev, 1)
            if nxt < NB:
                b1_stt(nxt, [0, 1])
            if prev >= 0:
                b3_evac(prev, 2)
                b3_evac(prev, 3)
            if nxt < NB:
                b1_stt(nxt, [2, 3])
                b1_sq(nxt)
            if prev >= 0:
                b3_evac(prev, 4)
            if nxt < NB:
                b1_h1b(nxt)
            if prev >= 0:
                b3_evac(prev, 5)
                b3_evac(prev, 6)
                b3_evac(prev, 7)
                b3_ln2stats(prev)
            for ft in range(NFT):
                hb = ps_h[ft % 2]
                hc = Hc[ft % 4]
                if ft == 0:
                    P.wait("tensor", "s_p1e", 2 * (b + 1))
                    if b == 0:
                        P.wait("tensor", "s_wv1", 16)
                        P.wait("tensor", "s_wu2", 16)
                        P.wait("scalar", "s_wb1", 16)
                if ft >= 2:
                    P.wait("tensor", "s_gelu", 32 * b + ft - 1)
                for rc in range(2):
                    col = rc * DFF + ft * 128
                    P.do("tensor", lambda e, o=hb.ap()[:, 0:512],
                         l=V1sb.ap()[:, col:col + 128],
                         r=p1sb.ap()[:, rc * 512:(rc + 1) * 512],
                         st=(rc == 0), sp=(rc == 1):
                         e.matmul(o, l, r, start=st, stop=sp),
                         inc="s_hmm" if rc == 1 else None)
                hv = P.cnt["s_hmm"]
                if ft == 24 and nxt < NB:
                    ln_stats_scl(1, nxt)
                P.wait("scalar", "s_hmm", hv)
                if ft >= 4 or b >= 1:
                    P.wait("scalar", "s_q2m", 64 * b + 2 * ft - 6)
                P.do("scalar", lambda e, hc=hc, hb=hb, ft=ft: e.activation(
                    hc[:, :], hb.ap()[:, 0:512], ACT.Gelu_apprx_tanh,
                    bias=b1sb.ap()[:, ft:ft + 1]), inc="s_gelu")
                if prev >= 0 and ft % 4 == 2:
                    b3_o2mm(prev, ft // 4)
                if ft >= 1:
                    qft = ft - 1
                    P.wait("tensor", "s_gelu", 32 * b + qft + 1)
                    for r2t in range(2):
                        col = qft * RK + r2t * 128
                        P.do("tensor", lambda e, o=ps_q2[r2t].ap()[:, 0:512],
                             l=U2sb.ap()[:, col:col + 128], r=Hc[qft % 4][:, :],
                             st=(qft == 0):
                             e.matmul(o, l, r, start=st, stop=False),
                             inc="s_q2m")
            P.wait("tensor", "s_gelu", 32 * b + 32)
            for r2t in range(2):
                col = 31 * RK + r2t * 128
                P.do("tensor", lambda e, o=ps_q2[r2t].ap()[:, 0:512],
                     l=U2sb.ap()[:, col:col + 128], r=Hc[31 % 4][:, :]:
                     e.matmul(o, l, r, start=False, stop=True),
                     inc="s_q2m")
            if prev >= 0:
                ln_stats_scl(2, prev)

        def emit_B23_tail(b):
            prev, nxt = b - 1, b + 1
            P.wait("vector", "s_q2m", 64 * (b + 1))
            if prev >= 0:
                P.wait("vector", "s_o2m", 8 * b)
            for r2t in range(2):
                P.do("vector", lambda e, r2t=r2t: e.tensor_copy(
                    q2sb.ap()[:, r2t * 512:(r2t + 1) * 512],
                    ps_q2[r2t].ap()[:, 0:512]), inc="s_q2e")
            if prev >= 0:
                b3_ln2apply(prev)
            if nxt < NB:
                b1_h1(nxt)
            if prev >= 0:
                b3_store(prev)

        def emit_B3tail():
            b = NB - 1
            for oi in range(8):
                b3_o2mm(b, oi)
            for oi in range(8):
                b3_evac(b, oi, defer=True)
            for tt in range(NTT):
                b3_stats_tt(b, tt)
            b3_ln2stats(b)
            b3_ln2apply(b)
            ln_stats_scl(2, b)
            b3_store(b)

        emit_B1_loads(0)
        b1_stt(0, [0, 1, 2, 3])
        b1_sq(0)
        ln_stats_scl(1, 0)
        b1_h1b(0)
        emit_B1T(0)
        b1_h1(0)
        for b in range(NB):
            emit_B2p1(b)
            if b + 1 < NB:
                emit_B1_loads(b + 1)
            emit_B23(b)
            if b + 1 < NB:
                emit_B1T(b + 1)
            emit_B23_tail(b)
        emit_B3tail()

        P.wait("sync", "s_y", 16 * NB)

        blockB = nc.Block()
        with blockB as blk:
            plan.emit(blk)

    return nc


_cache = {}


def kernel(L0_attn_x, L0_attn_Q, L0_attn_K, L0_attn_VO,
           L0_ffn_U1, L0_ffn_V1, L0_ffn_b1, L0_ffn_U2, L0_ffn_V2, L0_ffn_b2):
    xs = np.ascontiguousarray(np.asarray(L0_attn_x, dtype=np.float32))
    Q = np.ascontiguousarray(np.asarray(L0_attn_Q, dtype=np.float32))
    K = np.ascontiguousarray(np.asarray(L0_attn_K, dtype=np.float32))
    VO = np.ascontiguousarray(np.asarray(L0_attn_VO, dtype=np.float32))
    U1 = np.ascontiguousarray(np.asarray(L0_ffn_U1, dtype=np.float32))
    V1 = np.ascontiguousarray(np.asarray(L0_ffn_V1, dtype=np.float32))
    b1 = np.ascontiguousarray(np.asarray(L0_ffn_b1, dtype=np.float32))
    U2 = np.ascontiguousarray(np.asarray(L0_ffn_U2, dtype=np.float32))
    V2 = np.ascontiguousarray(np.asarray(L0_ffn_V2, dtype=np.float32))
    b2 = np.ascontiguousarray(np.asarray(L0_ffn_b2, dtype=np.float32))

    if "nc" not in _cache:
        _cache["nc"] = build_nc()
    nc = _cache["nc"]

    in_maps = []
    for c in range(8):
        bidx, off = c // 2, (c % 2) * TQ
        xb = np.ascontiguousarray(
            np.concatenate([xs[bidx, off:], xs[bidx, :off]], axis=0))
        in_maps.append({
            "x": xb, "q": Q, "k": K, "vo": VO,
            "u1": U1, "v1": V1, "b1": b1, "u2": U2, "v2": V2, "b2": b2,
        })
    res = run_bass_kernel_spmd(nc, in_maps, core_ids=list(range(8)))
    out = np.empty((4, T, D), dtype=np.float32)
    for c in range(8):
        bidx, off = c // 2, (c % 2) * TQ
        out[bidx, off:off + TQ] = res.results[c]["y"]
    return out
